# version 07
# baseline (speedup 1.0000x reference)
import numpy as np
import ml_dtypes
import concourse.bass as bass
import concourse.mybir as mybir
from concourse.bass_utils import run_bass_kernel_spmd
from contextlib import ExitStack

F32 = mybir.dt.float32
BF16 = mybir.dt.bfloat16
I32 = mybir.dt.int32
AF = mybir.ActivationFunctionType
ALU = mybir.AluOpType
AX = mybir.AxisListType

NCORE = 8
S = 8192
DM = 2048
HD = 128
SUB = 64
NQT = S // 128
TC5 = 512
NC5 = S // TC5
FFT = 2048
NFF = S // FFT
RST = 1024
NRS = S // RST
NLOC = 2048
KC = 16
NS = 16
MCT = 16
THETA = 10000.0
BF = ml_dtypes.bfloat16


def build_nc(dump=(), detect_races=True, max_phase=5, attn_mode='full',
             qt_list=None):
    dump = set(dump)
    nc = bass.Bass("TRN2", target_bir_lowering=False,
                   detect_race_conditions=detect_races)

    x = nc.dram_tensor("x", [S, DM], F32, kind="ExternalInput").ap()
    xresT = nc.dram_tensor("xresT", [256, S], F32, kind="ExternalInput").ap()
    wq = nc.dram_tensor("wq", [DM, 256], F32, kind="ExternalInput").ap()
    wk = nc.dram_tensor("wk", [DM, HD], F32, kind="ExternalInput").ap()
    wv = nc.dram_tensor("wv", [DM, HD], F32, kind="ExternalInput").ap()
    wo = nc.dram_tensor("wo", [256, DM], F32, kind="ExternalInput").ap()
    lam2 = nc.dram_tensor("lam2", [2, 128], F32, kind="ExternalInput").ap()
    gpk = nc.dram_tensor("gpk", [NLOC, 1024], I32, kind="ExternalInput").ap()
    gsc = nc.dram_tensor("gsc", [NLOC, 64], F32, kind="ExternalInput").ap()
    upk = nc.dram_tensor("upk", [NLOC, 1024], I32, kind="ExternalInput").ap()
    usc = nc.dram_tensor("usc", [NLOC, 64], F32, kind="ExternalInput").ap()
    dpk = nc.dram_tensor("dpk", [DM, 1024], I32, kind="ExternalInput").ap()
    dsc = nc.dram_tensor("dsc", [DM, 64], F32, kind="ExternalInput").ap()
    rcosq = nc.dram_tensor("rcosq", [128, S], F32, kind="ExternalInput").ap()
    rsinq = nc.dram_tensor("rsinq", [128, S], F32, kind="ExternalInput").ap()
    rcosk = nc.dram_tensor("rcosk", [128, S], F32, kind="ExternalInput").ap()
    rsink = nc.dram_tensor("rsink", [128, S], F32, kind="ExternalInput").ap()
    ident = nc.dram_tensor("ident", [128, 128], BF16, kind="ExternalInput").ap()
    mask_st = nc.dram_tensor("mask_st", [128, 640], F32, kind="ExternalInput").ap()
    mask_ed = nc.dram_tensor("mask_ed", [4, 128, 512], F32, kind="ExternalInput").ap()

    xnT = nc.dram_tensor("xnT", [DM, S], BF16).ap()
    qTd = nc.dram_tensor("qTd", [NQT, 2, 2, 64, 128], BF16).ap()
    wgTd = nc.dram_tensor("wgTd", [NS * KC, 128, 128], BF16).ap()
    wuTd = nc.dram_tensor("wuTd", [NS * KC, 128, 128], BF16).ap()
    wdTd = nc.dram_tensor("wdTd", [MCT * NS, 128, 128], BF16).ap()
    Cb = [nc.dram_tensor(f"Cbuf{j}", [DM, RST], F32).ap() for j in range(NRS)]
    ccdum_in = nc.dram_tensor("ccdum_in", [8, 128], F32).ap()
    ccdum_out = nc.dram_tensor("ccdum_out", [1, 128], F32).ap()
    Scb = [nc.dram_tensor(f"Scbuf{j}", [256, RST], F32).ap() for j in range(NRS)]

    out = nc.dram_tensor("out", [256, S], F32, kind="ExternalOutput").ap()

    dump_specs = []
    def add_dump(name, src, shape):
        d = nc.dram_tensor(name, shape, src.dtype if hasattr(src, "dtype") else BF16,
                           kind="ExternalOutput").ap()
        dump_specs.append((d, src))
        return d
    if "xnT" in dump:
        add_dump("d_xnT", xnT, [DM, S])
    if "qTd" in dump:
        add_dump("d_qTd", qTd.rearrange("a b s c d -> (a b s c) d"),
                 [NQT * 2 * 2 * 64, 128])
    if "wgTd" in dump:
        add_dump("d_wgTd", wgTd.rearrange("a b c -> (a b) c"), [NS * KC * 128, 128])
    if "wuTd" in dump:
        add_dump("d_wuTd", wuTd.rearrange("a b c -> (a b) c"), [NS * KC * 128, 128])
    if "wdTd" in dump:
        add_dump("d_wdTd", wdTd.rearrange("a b c -> (a b) c"), [MCT * NS * 128, 128])
    if "C" in dump:
        for j in range(NRS):
            add_dump(f"d_C{j}", Cb[j], [DM, RST])
    want_kv = "kv" in dump

    es = ExitStack()
    with es:
        T = lambda n, s, d: es.enter_context(nc.sbuf_tensor(n, s, d))
        SEM = lambda n: es.enter_context(nc.semaphore(n))

        cst = T("cst", [128, 8], F32)
        ident_sb = T("ident_sb", [128, 128], BF16)
        lam_sb = T("lam_sb", [128, 2], F32)

        for i, val in enumerate([7.5, -4.0, -6.0, -1.0, 1e-5, -0.5, 0.0, 1.0]):
            nc.const_aps.register(F32, val, cst[:, i:i + 1])

        ld0 = SEM("ld0"); cm0 = SEM("cm0")
        s_x = SEM("s_x"); s_st1 = SEM("s_st1"); s_sq = SEM("s_sq")
        s_rst = SEM("s_rst"); s_xn = SEM("s_xn"); s_tr = SEM("s_tr")
        s_ev = SEM("s_ev"); s_stg = SEM("s_stg"); s_w = SEM("s_w")
        s_xc = SEM("s_xc"); s_rl = SEM("s_rl"); s_mmq = SEM("s_mmq")
        s_mmk = SEM("s_mmk"); s_mmv = SEM("s_mmv"); s_rq = SEM("s_rq")
        s_rk = SEM("s_rk"); s_ve = SEM("s_ve"); s_qs = SEM("s_qs")
        s_pk = SEM("s_pk"); s_dv = SEM("s_dv"); s_ac = SEM("s_ac")
        s_tw = SEM("s_tw"); s_we = SEM("s_we"); s_ws = SEM("s_ws")
        s_qb = SEM("s_qb"); s_sc = SEM("s_sc"); s_sm = SEM("s_sm")
        s_ax = SEM("s_ax"); s_dt = SEM("s_dt"); s_de = SEM("s_de")
        s_av = SEM("s_av"); s_ae = SEM("s_ae"); s_wo = SEM("s_wo")
        s_ce = SEM("s_ce"); s_cw = SEM("s_cw")
        s_xf = SEM("s_xf"); s_wg = SEM("s_wg"); s_wd = SEM("s_wd")
        s_gm = SEM("s_gm"); s_um = SEM("s_um"); s_gl = SEM("s_gl")
        s_iv = SEM("s_iv"); s_dm = SEM("s_dm"); s_dsg = SEM("s_dsg")
        s_ca = SEM("s_ca"); s_cc = SEM("s_cc"); s_sl = SEM("s_sl")
        s_xr = SEM("s_xr"); s_oa = SEM("s_oa"); s_ow = SEM("s_ow")
        s_nm = SEM("s_nm"); s_ex = SEM("s_ex"); s_ch = SEM("s_ch")
        s_ccd = SEM("s_ccd")
        s_e2a = SEM("s_e2a"); s_dfn = SEM("s_dfn")
        s_dbg = SEM("s_dbg")

        attn_es = ExitStack()
        Ta = lambda n, s, d: attn_es.enter_context(nc.sbuf_tensor(n, s, d))
        mst_sb = Ta("mst_sb", [128, 640], F32)
        med_sb = Ta("med_sb", [128, 4 * 512], F32)
        kT1_sb = Ta("kT1_sb", [64, S], BF16)
        kT2_sb = Ta("kT2_sb", [64, S], BF16)
        v_sb = Ta("v_sb", [128, 64 * 128], BF16)
        wq_sb = Ta("wq_sb", [128, KC * 256], BF16)
        wk_sb = Ta("wk_sb", [128, KC * 128], BF16)
        wv_sb = Ta("wv_sb", [128, KC * 128], BF16)
        wo_sb = Ta("wo_sb", [128, 2 * DM], BF16)

        with nc.Block() as b0:
            @b0.vector
            def _(v):
                vals = [7.5, -4.0, -6.0, -1.0, 1e-5, -0.5, 0.0, 1.0]
                for i, val in enumerate(vals[:-1]):
                    v.memset(cst[:, i:i + 1], val)
                v.memset(cst[:, 7:8], vals[-1]).then_inc(cm0, 1)

            @b0.sync
            def _(sy):
                sy.dma_start(out=ident_sb[:], in_=ident).then_inc(ld0, 16)
                sy.dma_start(out=mst_sb[:], in_=mask_st).then_inc(ld0, 16)
                sy.dma_start(out=med_sb.ap().rearrange("p (e w) -> p e w", e=4),
                             in_=mask_ed.transpose([1, 0, 2])).then_inc(ld0, 16)
                sy.dma_start(out=lam_sb.ap().unsqueeze(2),
                             in_=lam2.transpose([1, 0]).unsqueeze(2),
                             allow_slow_non_contiguous=True).then_inc(ld0, 16)
        LD0 = 16 * 4
        nc.all_engine_barrier()

        if max_phase >= 1:
            with ExitStack() as ph:
                Tl = lambda n, s, d: ph.enter_context(nc.sbuf_tensor(n, s, d))
                Pl = lambda n, s, d: ph.enter_context(nc.psum_tensor(n, s, d))
                xb = [Tl(f"xb{i}", [128, DM], F32) for i in range(4)]
                sq_scr = Tl("sq_scr", [128, DM], F32)
                stat = Tl("stat", [128, 32], F32)
                xnb = [Tl(f"xnb{i}", [128, DM], BF16) for i in range(2)]
                xnstage = Tl("xnstage", [128, KC * TC5], BF16)
                ps_tr = [Pl(f"ps_tr{i}", [128, 128], BF16) for i in range(4)]

                with nc.Block() as b1:
                    @b1.gpsimd
                    def _(g):
                        for kc in range(KC):
                            g.dma_start(out=wq_sb[:, kc * 256:(kc + 1) * 256],
                                        in_=wq[kc * 128:(kc + 1) * 128, :]).then_inc(s_w, 16)
                            g.dma_start(out=wk_sb[:, kc * 128:(kc + 1) * 128],
                                        in_=wk[kc * 128:(kc + 1) * 128, :]).then_inc(s_w, 16)
                            g.dma_start(out=wv_sb[:, kc * 128:(kc + 1) * 128],
                                        in_=wv[kc * 128:(kc + 1) * 128, :]).then_inc(s_w, 16)
                        for c in range(2):
                            g.dma_start(out=wo_sb[:, c * DM:(c + 1) * DM],
                                        in_=wo[c * 128:(c + 1) * 128, :]).then_inc(s_w, 16)

                    @b1.sync
                    def _(sy):
                        for tc in range(NC5):
                            for s in range(4):
                                i = 4 * tc + s
                                if i >= 4:
                                    sy.wait_ge(s_xn, i - 3)
                                sy.dma_start(out=xb[s][:],
                                             in_=x[i * 128:(i + 1) * 128, :]
                                             ).then_inc(s_x, 16)
                            if tc > 0:
                                sy.wait_ge(s_ev, 64 * tc)
                                for kc in range(KC):
                                    sy.dma_start(
                                        out=xnT[kc * 128:(kc + 1) * 128,
                                                (tc - 1) * TC5:tc * TC5],
                                        in_=xnstage[:, kc * TC5:(kc + 1) * TC5]
                                        ).then_inc(s_stg, 16)
                        sy.wait_ge(s_ev, 64 * NC5)
                        for kc in range(KC):
                            sy.dma_start(out=xnT[kc * 128:(kc + 1) * 128, (NC5 - 1) * TC5:],
                                         in_=xnstage[:, kc * TC5:(kc + 1) * TC5]
                                         ).then_inc(s_stg, 16)
                        sy.wait_ge(s_stg, 16 * KC * NC5)

                    @b1.vector
                    def _(v):
                        v.wait_ge(cm0, 1)
                        for tc in range(NC5):
                            for s in range(4):
                                v.wait_ge(s_x, 16 * (4 * tc + s + 1))
                                v.reduce_sum(stat[:, 8 * s:8 * s + 1], xb[s][:], axis=AX.X)
                            for s in range(4):
                                v.tensor_scalar_mul(stat[:, 8 * s + 2:8 * s + 3],
                                                    stat[:, 8 * s:8 * s + 1], 1.0 / DM)
                            for s in range(4):
                                v.tensor_mul(stat[:, 8 * s + 3:8 * s + 4],
                                             stat[:, 8 * s + 2:8 * s + 3],
                                             stat[:, 8 * s + 2:8 * s + 3])
                            v.wait_ge(s_sq, 4 * (tc + 1))
                            for s in range(4):
                                v.scalar_tensor_tensor(
                                    out=stat[:, 8 * s + 4:8 * s + 5],
                                    in0=stat[:, 8 * s + 1:8 * s + 2], scalar=1.0 / DM,
                                    in1=stat[:, 8 * s + 3:8 * s + 4],
                                    op0=ALU.mult, op1=ALU.subtract)
                            v.tensor_copy(stat[:, 6:7] if False else stat[:, 8 * 0 + 6:8 * 0 + 7],
                                          stat[:, 8 * 0 + 4:8 * 0 + 5]) if False else None
                            v.tensor_mul(stat[:, 8 * 3 + 6:8 * 3 + 7],
                                         stat[:, 8 * 0 + 4:8 * 0 + 5],
                                         cst[:, 7:8]).then_inc(s_st1, 1)
                            v.wait_ge(s_rst, tc + 1)
                            for s in range(4):
                                v.scalar_tensor_tensor(
                                    out=stat[:, 8 * s + 3:8 * s + 4],
                                    in0=stat[:, 8 * s + 2:8 * s + 3], scalar=-1.0,
                                    in1=stat[:, 8 * s + 5:8 * s + 6],
                                    op0=ALU.mult, op1=ALU.mult)
                            for s in range(4):
                                i = 4 * tc + s
                                if i >= 2:
                                    v.wait_ge(s_tr, 16 * (i - 1))
                                v.tensor_scalar(
                                    out=xnb[s % 2][:], in0=xb[s][:],
                                    scalar1=stat[:, 8 * s + 5:8 * s + 6],
                                    scalar2=stat[:, 8 * s + 3:8 * s + 4],
                                    op0=ALU.mult, op1=ALU.add).then_inc(s_xn, 1)

                    @b1.scalar
                    def _(sc):
                        sc.wait_ge(cm0, 1)
                        for tc in range(NC5):
                            if tc > 0:
                                sc.wait_ge(s_st1, tc)
                            for s in range(4):
                                sc.wait_ge(s_x, 16 * (4 * tc + s + 1))
                                sc.activation(sq_scr[:], xb[s][:], AF.Square,
                                              accum_out=stat[:, 8 * s + 1:8 * s + 2]
                                              ).then_inc(s_sq, 1)
                            sc.wait_ge(s_st1, tc + 1)
                            for s in range(4):
                                sc.activation(stat[:, 8 * s + 4:8 * s + 5],
                                              stat[:, 8 * s + 4:8 * s + 5],
                                              AF.Ln, bias=1e-5)
                            for s in range(4):
                                sc.activation(stat[:, 8 * s + 5:8 * s + 6],
                                              stat[:, 8 * s + 4:8 * s + 5],
                                              AF.Exp, scale=-0.5)
                            sc.copy(stat[:, 8 * 3 + 6:8 * 3 + 7],
                                    stat[:, 8 * 0 + 5:8 * 0 + 6]).then_inc(s_rst, 1)
                            if tc > 0:
                                sc.wait_ge(s_stg, 16 * KC * tc)
                            for s in range(4):
                                for kc in range(KC):
                                    n = 64 * tc + 16 * s + kc + 1
                                    sc.wait_ge(s_tr, n)
                                    sc.copy(xnstage[:, kc * TC5 + s * 128:
                                                    kc * TC5 + (s + 1) * 128],
                                            ps_tr[(n - 1) % 4][:]).then_inc(s_ev, 1)

                    @b1.tensor
                    def _(t):
                        t.wait_ge(ld0, LD0)
                        n = 0
                        for tc in range(NC5):
                            for s in range(4):
                                t.wait_ge(s_xn, 4 * tc + s + 1)
                                for kc in range(KC):
                                    n += 1
                                    if n > 4:
                                        t.wait_ge(s_ev, n - 4)
                                    t.transpose(ps_tr[(n - 1) % 4][:],
                                                xnb[s % 2][:, kc * 128:(kc + 1) * 128],
                                                ident_sb[:]).then_inc(s_tr, 1)
        nc.all_engine_barrier()

        if max_phase >= 2:
            with ExitStack() as ph:
                Tl = lambda n, s, d: ph.enter_context(nc.sbuf_tensor(n, s, d))
                Pl = lambda n, s, d: ph.enter_context(nc.psum_tensor(n, s, d))
                xns = [Tl(f"xns{i}", [128, KC * TC5], BF16) for i in range(2)]
                rcq = Tl("rcq", [128, TC5], F32)
                rsq = Tl("rsq", [128, TC5], F32)
                rck = Tl("rck", [128, TC5], F32)
                rsk = Tl("rsk", [128, TC5], F32)
                rtmp = [Tl(f"rtmp{i}", [64, TC5], F32) for i in range(4)]
                qkst = [Tl(f"qkst{i}", [128, TC5], BF16) for i in range(2)]
                ps_q = [Pl(f"ps_q{i}", [128, TC5], F32) for i in range(2)]
                ps_k = Pl("ps_k", [128, TC5], F32)
                ps_v = [Pl(f"ps_v{i}", [128, 128], F32) for i in range(2)]

                with nc.Block() as b2:
                    @b2.gpsimd
                    def _(g):
                        for tc in range(NC5):
                            for h in range(2):
                                g.wait_ge(s_rq, 2 * tc + h + 1)
                                for sb in range(2):
                                    g.dma_start(
                                        out=qTd[tc * 4:(tc + 1) * 4, h, sb]
                                        .transpose([1, 0, 2]),
                                        in_=qkst[h].ap()[sb * 64:(sb + 1) * 64, :]
                                        .rearrange("p (q f) -> p q f", q=4)
                                        ).then_inc(s_qs, 16)
                        g.wait_ge(s_qs, 32 * 2 * NC5)

                    @b2.sync
                    def _(sy):
                        for tc in range(NC5):
                            if tc >= 2:
                                sy.wait_ge(s_mmv, 4 * (tc - 1))
                            for kc in range(KC):
                                sy.dma_start(out=xns[tc % 2][:, kc * TC5:(kc + 1) * TC5],
                                             in_=xnT[kc * 128:(kc + 1) * 128,
                                                     tc * TC5:(tc + 1) * TC5]
                                             ).then_inc(s_xc, 16)
                            if tc >= 1:
                                sy.wait_ge(s_rq, 2 * tc)
                                sy.wait_ge(s_rk, tc)
                            sl = slice(tc * TC5, (tc + 1) * TC5)
                            sy.dma_start(out=rcq[:], in_=rcosq[:, sl]).then_inc(s_rl, 16)
                            sy.dma_start(out=rsq[:], in_=rsinq[:, sl]).then_inc(s_rl, 16)
                            sy.dma_start(out=rck[:], in_=rcosk[:, sl]).then_inc(s_rl, 16)
                            sy.dma_start(out=rsk[:], in_=rsink[:, sl]).then_inc(s_rl, 16)

                    @b2.tensor
                    def _(t):
                        t.wait_ge(s_w, 16 * (3 * KC + 2))
                        for tc in range(NC5):
                            t.wait_ge(s_xc, 16 * KC * (tc + 1))
                            for h in range(2):
                                if tc > 0:
                                    t.wait_ge(s_rq, 2 * (tc - 1) + h + 1)
                                for kc in range(KC):
                                    inst = t.matmul(ps_q[h][:],
                                             wq_sb[:, kc * 256 + h * 128:
                                                   kc * 256 + (h + 1) * 128],
                                             xns[tc % 2][:, kc * TC5:(kc + 1) * TC5],
                                             start=(kc == 0), stop=(kc == KC - 1))
                                inst.then_inc(s_mmq, 1)
                            if tc > 0:
                                t.wait_ge(s_rk, tc)
                            for kc in range(KC):
                                inst = t.matmul(ps_k[:],
                                         wk_sb[:, kc * 128:(kc + 1) * 128],
                                         xns[tc % 2][:, kc * TC5:(kc + 1) * TC5],
                                         start=(kc == 0), stop=(kc == KC - 1))
                            inst.then_inc(s_mmk, 1)
                            for s in range(4):
                                i = 4 * tc + s
                                if i >= 2:
                                    t.wait_ge(s_ve, i - 1)
                                for kc in range(KC):
                                    inst = t.matmul(ps_v[i % 2][:],
                                             xns[tc % 2][:, kc * TC5 + s * 128:
                                                          kc * TC5 + (s + 1) * 128],
                                             wv_sb[:, kc * 128:(kc + 1) * 128],
                                             start=(kc == 0), stop=(kc == KC - 1))
                                inst.then_inc(s_mmv, 1)

                    @b2.vector
                    def _(v):
                        def rope(dst_lo, dst_hi, ps, ct, st):
                            v.tensor_mul(rtmp[0][:], ps[0:64, :], ct[0:64, :])
                            v.tensor_mul(rtmp[1][:], ps[64:128, :], st[0:64, :])
                            v.tensor_mul(rtmp[2][:], ps[64:128, :], ct[64:128, :])
                            v.tensor_mul(rtmp[3][:], ps[0:64, :], st[64:128, :])
                            v.tensor_sub(dst_lo, rtmp[0][:], rtmp[1][:])
                            return v.tensor_add(dst_hi, rtmp[2][:], rtmp[3][:])

                        for tc in range(NC5):
                            v.wait_ge(s_rl, 16 * 4 * (tc + 1))
                            for h in range(2):
                                v.wait_ge(s_mmq, 2 * tc + h + 1)
                                if tc > 0:
                                    v.wait_ge(s_qs, 32 * (2 * (tc - 1) + h + 1))
                                rope(qkst[h][0:64, :], qkst[h][64:128, :],
                                     ps_q[h], rcq, rsq).then_inc(s_rq, 1)
                            v.wait_ge(s_mmk, tc + 1)
                            rope(kT1_sb[:, tc * TC5:(tc + 1) * TC5],
                                 kT2_sb[:, tc * TC5:(tc + 1) * TC5],
                                 ps_k, rck, rsk).then_inc(s_rk, 1)

                    @b2.scalar
                    def _(sc):
                        for tc in range(NC5):
                            for s in range(4):
                                i = 4 * tc + s
                                sc.wait_ge(s_mmv, i + 1)
                                sc.copy(v_sb[:, i * 128:(i + 1) * 128],
                                        ps_v[i % 2][:]).then_inc(s_ve, 1)
        nc.all_engine_barrier()

        if max_phase >= 3:
            with ExitStack() as ph:
                Tl = lambda n, s, d: ph.enter_context(nc.sbuf_tensor(n, s, d))
                Pl = lambda n, s, d: ph.enter_context(nc.psum_tensor(n, s, d))
                pkt = [Tl(f"pkt{i}", [128, 1024], I32) for i in range(2)]
                scs = [Tl(f"scs{i}", [128, 64], F32) for i in range(2)]
                lo_i = Tl("lo_i", [128, 1024], I32)
                hi_i = Tl("hi_i", [128, 1024], I32)
                cbf = Tl("cbf", [128, 2048], BF16)
                sgn = Tl("sgn", [128, 2048], BF16)
                mtl = Tl("mtl", [128, 2048], BF16)
                rr1 = Tl("rr1", [128, 2048], BF16)
                rr2 = Tl("rr2", [128, 2048], BF16)
                fvl = Tl("fvl", [128, 2048], BF16)
                svl = Tl("svl", [128, 2048], BF16)
                wstg = [Tl(f"wstg{i}", [128, KC * 128], BF16) for i in range(2)]
                ps_w = [Pl(f"ps_w{i}", [128, 128], BF16) for i in range(4)]

                mats = [(gpk, gsc, wgTd), (upk, usc, wuTd), (dpk, dsc, wdTd)]
                NT = 16

                with nc.Block() as b3:
                    @b3.sync
                    def _(sy):
                        n = 0
                        for (pk, sc_, wd) in mats:
                            for tt in range(NT):
                                n += 1
                                if n > 2:
                                    sy.wait_ge(s_dv, 8 * (n - 2))
                                sy.dma_start(out=pkt[n % 2][:],
                                             in_=pk[tt * 128:(tt + 1) * 128, :]
                                             ).then_inc(s_pk, 16)
                                sy.dma_start(out=scs[n % 2][:],
                                             in_=sc_[tt * 128:(tt + 1) * 128, :]
                                             ).then_inc(s_pk, 16)
                                sy.wait_ge(s_we, 16 * n)
                                sy.dma_start(out=wd[tt * KC:(tt + 1) * KC]
                                             .transpose([1, 0, 2]),
                                             in_=wstg[n % 2].ap()
                                             .rearrange("p (b f) -> p b f", b=KC)
                                             ).then_inc(s_ws, 16)
                        sy.wait_ge(s_ws, 16 * 3 * NT)

                    @b3.vector
                    def _(v):
                        n = 0
                        for mi in range(3):
                            for tt in range(NT):
                                n += 1
                                v.wait_ge(s_pk, 32 * n)
                                v.tensor_single_scalar(lo_i[:], pkt[n % 2][:], 15,
                                                       ALU.bitwise_and)
                                v.tensor_single_scalar(hi_i[:], pkt[n % 2][:], 4,
                                                       ALU.logical_shift_right
                                                       ).then_inc(s_dv, 1)
                                v.wait_ge(s_dv, 8 * (n - 1) + 1)
                                v.tensor_copy(cbf[:, 0::2], lo_i[:])
                                v.tensor_copy(cbf[:, 1::2], hi_i[:]).then_inc(s_dv, 1)
                                v.wait_ge(s_ac, 3 * (n - 1) + 1)
                                v.tensor_scalar(out=mtl[:], in0=sgn[:],
                                                scalar1=4.0, scalar2=-4.0,
                                                op0=ALU.mult, op1=ALU.add
                                                ).then_inc(s_dv, 1)
                                v.wait_ge(s_dv, 8 * (n - 1) + 3)
                                v.tensor_add(mtl[:], cbf[:], mtl[:]).then_inc(s_dv, 1)
                                v.wait_ge(s_ac, 3 * n)
                                v.scalar_tensor_tensor(out=rr1[:], in0=rr1[:],
                                                       scalar=0.5, in1=rr2[:],
                                                       op0=ALU.mult, op1=ALU.add
                                                       ).then_inc(s_dv, 1)
                                v.wait_ge(s_dv, 8 * (n - 1) + 5)
                                v.scalar_tensor_tensor(out=fvl[:], in0=mtl[:],
                                                       scalar=0.5, in1=rr1[:],
                                                       op0=ALU.mult, op1=ALU.add
                                                       ).then_inc(s_dv, 1)
                                v.wait_ge(s_dv, 8 * (n - 1) + 6)
                                v.tensor_mul(fvl[:], fvl[:], sgn[:]).then_inc(s_dv, 1)
                                v.wait_ge(s_dv, 8 * (n - 1) + 7)
                                if n > 1:
                                    v.wait_ge(s_tw, 16 * (n - 1))
                                v.tensor_tensor(
                                    out=svl.ap().rearrange("p (b k) -> p b k", b=64),
                                    in0=fvl.ap().rearrange("p (b k) -> p b k", b=64),
                                    in1=scs[n % 2].ap().unsqueeze(2)
                                        .broadcast_to([128, 64, 32]),
                                    op=ALU.mult).then_inc(s_dv, 1)

                    @b3.scalar
                    def _(sc):
                        n = 0
                        for mi in range(3):
                            for tt in range(NT):
                                n += 1
                                sc.wait_ge(s_dv, 8 * (n - 1) + 2)
                                sc.activation(sgn[:], cbf[:], AF.Sign,
                                              bias=7.5, scale=-1.0).then_inc(s_ac, 1)
                                sc.wait_ge(s_dv, 8 * (n - 1) + 4)
                                sc.activation(rr1[:], mtl[:], AF.Relu, bias=-4.0)
                                sc.activation(rr2[:], mtl[:], AF.Relu, bias=-6.0
                                              ).then_inc(s_ac, 2)
                                for kc in range(KC):
                                    k = KC * (n - 1) + kc + 1
                                    sc.wait_ge(s_tw, k)
                                    sc.copy(wstg[n % 2][:, kc * 128:(kc + 1) * 128],
                                            ps_w[(k - 1) % 4][:]).then_inc(s_we, 1)

                    @b3.tensor
                    def _(t):
                        n = 0
                        k = 0
                        for mi in range(3):
                            for tt in range(NT):
                                n += 1
                                t.wait_ge(s_dv, 8 * n)
                                if n > 2:
                                    t.wait_ge(s_ws, 16 * (n - 2))
                                for kc in range(KC):
                                    k += 1
                                    if k > 4:
                                        t.wait_ge(s_we, k - 4)
                                    t.transpose(ps_w[(k - 1) % 4][:],
                                                svl[:, kc * 128:(kc + 1) * 128],
                                                ident_sb[:]).then_inc(s_tw, 1)
        nc.all_engine_barrier()

        if max_phase >= 4:
            with ExitStack() as ph:
                Tl = lambda n, s, d: ph.enter_context(nc.sbuf_tensor(n, s, d))
                Pl = lambda n, s, d: ph.enter_context(nc.psum_tensor(n, s, d))
                qblk = [Tl(f"qblk{i}", [64, 512], BF16) for i in range(2)]
                e1 = [Tl(f"e1_{i}", [128, 640], F32) for i in range(2)]
                e2 = [Tl(f"e2_{i}", [128, 640], F32) for i in range(2)]
                e1m = [Tl(f"e1m_{i}", [128, 640], BF16) for i in range(2)]
                e2m = [Tl(f"e2m_{i}", [128, 640], BF16) for i in range(2)]
                e2a = [Tl(f"e2a_{i}", [128, 640], BF16) for i in range(2)]
                dra = [Tl(f"dra_{i}", [128, 640], BF16) for i in range(2)]
                dre = [Tl(f"dre_{i}", [128, 640], BF16) for i in range(2)]
                dfin = [Tl(f"dfin_{i}", [128, 640], BF16) for i in range(2)]
                dT = [Tl(f"dT_{i}", [128, 640], BF16) for i in range(2)]
                st2 = Tl("st2", [128, 16], F32)
                dscr = Tl("dscr", [128, 2], F32)
                avT = [Tl(f"avT{i}", [128, 2 * 512], BF16) for i in range(2)]
                cstg = [Tl(f"cstg{i}", [128, 512], F32) for i in range(2)]
                psm1 = [Pl(f"psm1_{i}", [128, 512], F32) for i in range(2)]
                psm2 = [Pl(f"psm2_{i}", [128, 512], F32) for i in range(2)]
                pstl = [Pl(f"pstl_{i}", [128, 512], F32) for i in range(2)]
                psad = [Pl(f"psad_{i}", [128, 512], F32) for i in range(2)]

                QTS = list(range(NQT)) if qt_list is None else list(qt_list)
                N = 2 * len(QTS)
                NG = len(QTS) // 4 if len(QTS) % 4 == 0 else 0

                def wparams(qt):
                    if qt < 4:
                        return (0, 128 * (qt + 1),
                                med_sb[:, qt * 512:qt * 512 + 128 * (qt + 1)])
                    return 128 * (qt - 4), 640, mst_sb[:, 0:640]

                def it(n):
                    qi = (n - 1) // 2
                    return qi, QTS[qi], (n - 1) % 2, (n - 1) % 2

                CH = 7

                with nc.Block() as b4:
                    @b4.gpsimd
                    def _(g):
                        for gp in range(16):
                            j, off = gp // 2, (gp % 2) * 512
                            for mc in range(MCT):
                                nn = MCT * gp + mc + 1
                                g.wait_ge(s_ce, nn)
                                g.dma_start(out=Cb[j][mc * 128:(mc + 1) * 128,
                                                      off:off + 512],
                                            in_=cstg[(nn - 1) % 2][:]).then_inc(s_cw, 16)
                        g.wait_ge(s_cw, 16 * MCT * 16)

                    @b4.sync
                    def _(sy):
                        for qi, qt in enumerate(QTS):
                            if qi >= 2:
                                sy.wait_ge(s_sc, 2 * (qi - 1))
                            sy.dma_start(out=qblk[qi % 2].ap()
                                         .rearrange("p (h s f) -> p h s f", h=2, s=2),
                                         in_=qTd[qt].transpose([2, 0, 1, 3])
                                         ).then_inc(s_qb, 16)

                    @b4.tensor
                    def _(t):
                        wo_groups = 0
                        for n in range(1, N + 3):
                            if n <= N:
                                qi, qt, h, st = it(n)
                                w0, W, _m = wparams(qt)
                                Wm = min(W, 512)
                                if h == 0:
                                    t.wait_ge(s_qb, 16 * (qi + 1))
                                if n > 2:
                                    t.wait_ge(s_ex, n - 2)
                                if st == 0 and wo_groups > 0:
                                    t.wait_ge(s_ce, MCT * wo_groups)
                                q = qblk[qi % 2]
                                for half, ps, kTh in ((0, psm1[st], kT1_sb),
                                                      (1, psm2[st], kT2_sb)):
                                    lh = q[:, (2 * h + half) * 128:
                                           (2 * h + half + 1) * 128]
                                    inst = t.matmul(ps[:, 0:Wm], lh,
                                                    kTh[:, w0:w0 + Wm],
                                                    start=True, stop=True)
                                    if W > 512:
                                        inst = t.matmul(
                                            pstl[st][:, half * 128:(half + 1) * 128],
                                            lh, kTh[:, w0 + 512:w0 + 640],
                                            start=True, stop=True)
                                inst.then_inc(s_sc, 1)
                            m = n - 1
                            if 1 <= m <= N:
                                qi, qt, h, st = it(m)
                                w0, W, _m2 = wparams(qt)
                                t.wait_ge(s_dfn, m)
                                if m > 2:
                                    t.wait_ge(s_de, m - 2)
                                    t.wait_ge(s_ae, m - 2)
                                for w in range(W // 128):
                                    inst = t.transpose(
                                        psad[st].ap().bitcast(BF16)
                                        [:, 256 + w * 128:256 + (w + 1) * 128],
                                        dfin[st][:, w * 128:(w + 1) * 128],
                                        ident_sb[:])
                                inst.then_inc(s_dt, 1)
                            m = n - 2
                            if 1 <= m <= N:
                                qi, qt, h, st = it(m)
                                w0, W, _m2 = wparams(qt)
                                t.wait_ge(s_de, m)
                                for w in range(W // 128):
                                    inst = t.matmul(
                                        psad[st][:, 0:128],
                                        v_sb[:, (w0 // 128 + w) * 128:
                                             (w0 // 128 + w + 1) * 128],
                                        dT[st][:, w * 128:(w + 1) * 128],
                                        start=(w == 0), stop=(w == W // 128 - 1))
                                inst.then_inc(s_av, 1)
                                if m % 8 == 0:
                                    gloc = m // 8 - 1
                                    t.wait_ge(s_ae, 8 * (gloc + 1))
                                    t.wait_ge(s_ex, min(N, 8 * gloc + 9))
                                    for mc in range(MCT):
                                        nwo = MCT * gloc + mc + 1
                                        if nwo > 2:
                                            t.wait_ge(s_ce, nwo - 2)
                                        pw = psm1[0] if nwo % 2 == 1 else psm2[0]
                                        for c in range(2):
                                            inst = t.matmul(
                                                pw[:],
                                                wo_sb[:, c * DM + mc * 128:
                                                      c * DM + (mc + 1) * 128],
                                                avT[gloc % 2][:, c * 512:(c + 1) * 512],
                                                start=(c == 0), stop=(c == 1))
                                        inst.then_inc(s_wo, 1)
                                    wo_groups += 1

                    @b4.vector
                    def _(v):
                        for n in range(1, N + 4):
                            m = n - 3
                            if 1 <= m <= N:
                                qi, qt, h, st = it(m)
                                gi = qi // 4
                                v.wait_ge(s_av, m)
                                if qi % 4 == 0 and h == 0 and gi >= 2:
                                    v.wait_ge(s_wo, MCT * (gi - 1))
                                v.tensor_copy(avT[gi % 2][:, h * 512 + (qi % 4) * 128:
                                                          h * 512 + (qi % 4 + 1) * 128],
                                              psad[st][:, 0:128]).then_inc(s_ae, 1)
                                if m % 8 == 0:
                                    gloc = m // 8 - 1
                                    for mc in range(MCT):
                                        nwo = MCT * gloc + mc + 1
                                        v.wait_ge(s_wo, nwo)
                                        if nwo > 2:
                                            v.wait_ge(s_cw, 16 * (nwo - 2))
                                        pw = psm1[0] if nwo % 2 == 1 else psm2[0]
                                        v.tensor_copy(cstg[(nwo - 1) % 2][:], pw[:]
                                                      ).then_inc(s_ce, 1)

                            if n <= N:
                                qi, qt, h, st = it(n)
                                w0, W, msk = wparams(qt)
                                c0 = 8 * (st)
                                v.wait_ge(s_sc, n)
                                if W > 512:
                                    v.reduce_max(st2[:, c0:c0 + 1], psm1[st][:],
                                                 axis=AX.X, negate=True)
                                    v.reduce_max(st2[:, c0 + 1:c0 + 2], psm2[st][:],
                                                 axis=AX.X, negate=True)
                                    v.reduce_max(st2[:, c0 + 2:c0 + 3],
                                                 pstl[st][:, 0:128], axis=AX.X,
                                                 negate=True)
                                    v.reduce_max(st2[:, c0 + 3:c0 + 4],
                                                 pstl[st][:, 128:256], axis=AX.X,
                                                 negate=True)
                                    v.tensor_tensor(st2[:, c0:c0 + 1],
                                                    st2[:, c0:c0 + 1],
                                                    st2[:, c0 + 2:c0 + 3], ALU.min)
                                    v.tensor_tensor(st2[:, c0 + 1:c0 + 2],
                                                    st2[:, c0 + 1:c0 + 2],
                                                    st2[:, c0 + 3:c0 + 4], ALU.min)
                                    v.tensor_copy(dscr[:, 0:1], st2[:, c0:c0 + 1]
                                                  ).then_inc(s_nm, 1)
                                else:
                                    v.reduce_max(st2[:, c0:c0 + 1], psm1[st][:, 0:W],
                                                 axis=AX.X, negate=True)
                                    v.reduce_max(st2[:, c0 + 1:c0 + 2],
                                                 psm2[st][:, 0:W],
                                                 axis=AX.X, negate=True)
                                    v.tensor_copy(dscr[:, 0:1], st2[:, c0:c0 + 1]
                                                  ).then_inc(s_nm, 1)
                            m = n - 1
                            if 1 <= m <= N:
                                qi, qt, h, st = it(m)
                                w0, W, msk = wparams(qt)
                                c0 = 8 * st
                                base = CH * (m - 1)
                                v.wait_ge(s_ex, m)
                                if m > 2:
                                    v.wait_ge(s_e2a, m - 2)
                                v.scalar_tensor_tensor(
                                    out=e1m[st][:, 0:W], in0=e1[st][:, 0:W],
                                    scalar=1.0, in1=msk, op0=ALU.mult, op1=ALU.mult,
                                    accum_out=st2[:, c0 + 4:c0 + 5])
                                v.scalar_tensor_tensor(
                                    out=e2m[st][:, 0:W], in0=e2[st][:, 0:W],
                                    scalar=1.0, in1=msk, op0=ALU.mult, op1=ALU.mult,
                                    accum_out=st2[:, c0 + 5:c0 + 6]).then_inc(s_ch, 1)
                                v.wait_ge(s_ch, base + 1)
                                v.reciprocal(st2[:, c0 + 6:c0 + 7],
                                             st2[:, c0 + 5:c0 + 6])
                                v.reciprocal(st2[:, c0 + 2:c0 + 3],
                                             st2[:, c0 + 4:c0 + 5]).then_inc(s_ch, 1)
                                v.wait_ge(s_ch, base + 2)
                                v.tensor_mul(st2[:, c0 + 6:c0 + 7],
                                             st2[:, c0 + 6:c0 + 7],
                                             lam_sb[:, h:h + 1]).then_inc(s_ch, 1)
                                v.wait_ge(s_e2a, m)
                                v.scalar_tensor_tensor(
                                    out=dra[st][:, 0:W], in0=e1m[st][:, 0:W],
                                    scalar=st2[:, c0 + 2:c0 + 3], in1=e2a[st][:, 0:W],
                                    op0=ALU.mult, op1=ALU.subtract).then_inc(s_ch, 1)
                                v.wait_ge(s_ch, base + 4)
                                if m > 2:
                                    v.wait_ge(s_dfn, m - 2)
                                v.tensor_scalar(out=dre[st][:, 0:W],
                                                in0=dra[st][:, 0:W],
                                                scalar1=0.0, scalar2=None,
                                                op0=ALU.max, op1=ALU.add,
                                                accum_out=st2[:, c0 + 7:c0 + 8]
                                                ).then_inc(s_ch, 1)
                                v.wait_ge(s_ch, base + 5)
                                v.tensor_scalar(out=dscr[:, 1:2],
                                                in0=st2[:, c0 + 7:c0 + 8],
                                                scalar1=1e-6, scalar2=None,
                                                op0=ALU.add).then_inc(s_ch, 1)
                                v.wait_ge(s_ch, base + 6)
                                v.reciprocal(st2[:, c0 + 7:c0 + 8],
                                             dscr[:, 1:2]).then_inc(s_ch, 1)
                            m = n - 2
                            if 1 <= m <= N:
                                qi, qt, h, st = it(m)
                                w0, W, _m2 = wparams(qt)
                                v.wait_ge(s_dt, m)
                                if m > 2:
                                    v.wait_ge(s_av, m - 2)
                                v.tensor_copy(dT[st][:, 0:W],
                                              psad[st].ap().bitcast(BF16)
                                              [:, 256:256 + W]).then_inc(s_de, 1)
                    @b4.scalar
                    def _(sc):
                        for n in range(1, N + 2):
                            if n <= N:
                                qi, qt, h, st = it(n)
                                w0, W, msk = wparams(qt)
                                c0 = 8 * st
                                Wm = min(W, 512)
                                sc.wait_ge(s_nm, n)
                                if n > 2:
                                    sc.wait_ge(s_ch, CH * (n - 3) + 1)
                                sc.activation(e1[st][:, 0:Wm], psm1[st][:, 0:Wm],
                                              AF.Exp, bias=st2[:, c0:c0 + 1])
                                inst = sc.activation(e2[st][:, 0:Wm],
                                                     psm2[st][:, 0:Wm],
                                                     AF.Exp, bias=st2[:, c0 + 1:c0 + 2])
                                if W > 512:
                                    sc.activation(e1[st][:, 512:640],
                                                  pstl[st][:, 0:128],
                                                  AF.Exp, bias=st2[:, c0:c0 + 1])
                                    inst = sc.activation(e2[st][:, 512:640],
                                                         pstl[st][:, 128:256],
                                                         AF.Exp,
                                                         bias=st2[:, c0 + 1:c0 + 2])
                                inst.then_inc(s_ex, 1)
                            m = n - 1
                            if 1 <= m <= N:
                                qi, qt, h, st = it(m)
                                w0, W, msk = wparams(qt)
                                c0 = 8 * st
                                base = CH * (m - 1)
                                sc.wait_ge(s_ch, base + 3)
                                if m > 2:
                                    sc.wait_ge(s_ch, CH * (m - 3) + 4)
                                sc.mul(e2a[st][:, 0:W], e2m[st][:, 0:W],
                                       st2[:, c0 + 6:c0 + 7]).then_inc(s_e2a, 1)
                                sc.wait_ge(s_ch, base + 7)
                                if m > 2:
                                    sc.wait_ge(s_dt, m - 2)
                                sc.mul(dfin[st][:, 0:W], dre[st][:, 0:W],
                                       st2[:, c0 + 7:c0 + 8]).then_inc(s_dfn, 1)
        nc.all_engine_barrier()
        attn_es.close()

        if max_phase >= 5:
            with ExitStack() as ph:
                Tl = lambda n, s, d: ph.enter_context(nc.sbuf_tensor(n, s, d))
                Pl = lambda n, s, d: ph.enter_context(nc.psum_tensor(n, s, d))
                xnf = Tl("xnf", [128, KC * FFT], BF16)
                inter = Tl("inter", [128, NS * FFT], BF16)
                gst = Tl("gst", [128, FFT], BF16)
                wgb = [Tl(f"wgb{i}", [128, KC * 128], BF16) for i in range(2)]
                wub = [Tl(f"wub{i}", [128, KC * 128], BF16) for i in range(2)]
                wdb = [Tl(f"wdb{i}", [128, NS * 128], BF16) for i in range(2)]
                dstg = [Tl(f"dstg{i}", [128, FFT], F32) for i in range(2)]
                ccd_sb = Tl("ccd_sb", [8, 128], F32)
                ssb = [Tl(f"ssb{i}", [128, RST], F32) for i in range(2)]
                xq = [Tl(f"xq{i}", [128, RST], F32) for i in range(2)]
                osb = [Tl(f"osb{i}", [128, RST], F32) for i in range(2)]
                psf = [Pl(f"psf{i}", [128, FFT], F32) for i in range(2)]

                def asm_sync(sy, j):
                    sy.wait_ge(s_cc, j + 2)
                    for mt in range(2):
                        n = 2 * j + mt + 1
                        if n > 2:
                            sy.wait_ge(s_oa, n - 2)
                        sy.dma_start(out=ssb[n % 2][:],
                                     in_=Scb[j][mt * 128:(mt + 1) * 128, :]
                                     ).then_inc(s_sl, 16)
                        sy.dma_start(out=xq[n % 2][:],
                                     in_=xresT[mt * 128:(mt + 1) * 128,
                                               j * RST:(j + 1) * RST]
                                     ).then_inc(s_xr, 16)

                def asm_vec(v, j):
                    for mt in range(2):
                        n = 2 * j + mt + 1
                        v.wait_ge(s_sl, 16 * n)
                        v.wait_ge(s_xr, 16 * n)
                        if n > 2:
                            v.wait_ge(s_ow, 16 * (n - 2))
                        v.tensor_add(osb[n % 2][:], ssb[n % 2][:], xq[n % 2][:]
                                     ).then_inc(s_oa, 1)

                def asm_gps(g, j):
                    for mt in range(2):
                        n = 2 * j + mt + 1
                        g.wait_ge(s_oa, n)
                        g.dma_start(out=out[mt * 128:(mt + 1) * 128,
                                            j * RST:(j + 1) * RST],
                                    in_=osb[n % 2][:]).then_inc(s_ow, 16)

                with nc.Block() as b5:
                    @b5.sync
                    def _(sy):
                        for fc in range(NFF):
                            if fc > 0:
                                sy.wait_ge(s_um, NS * fc)
                            for kc in range(KC):
                                sy.dma_start(out=xnf[:, kc * FFT:(kc + 1) * FFT],
                                             in_=xnT[kc * 128:(kc + 1) * 128,
                                                     fc * FFT:(fc + 1) * FFT]
                                             ).then_inc(s_xf, 16)
                            for ns in range(NS):
                                n = NS * fc + ns + 1
                                if n > 2:
                                    sy.wait_ge(s_um, n - 2)
                                sy.dma_start(out=wgb[n % 2].ap()
                                             .rearrange("p (b f) -> p b f", b=KC),
                                             in_=wgTd[ns * KC:(ns + 1) * KC]
                                             .transpose([1, 0, 2])
                                             ).then_inc(s_wg, 16)
                                sy.dma_start(out=wub[n % 2].ap()
                                             .rearrange("p (b f) -> p b f", b=KC),
                                             in_=wuTd[ns * KC:(ns + 1) * KC]
                                             .transpose([1, 0, 2])
                                             ).then_inc(s_wg, 16)
                            for mc in range(MCT):
                                n = MCT * fc + mc + 1
                                if n > 2:
                                    sy.wait_ge(s_dm, n - 2)
                                sy.dma_start(out=wdb[n % 2].ap()
                                             .rearrange("p (b f) -> p b f", b=NS),
                                             in_=wdTd[mc * NS:(mc + 1) * NS]
                                             .transpose([1, 0, 2])
                                             ).then_inc(s_wd, 16)
                            if fc > 0:
                                for j in (2 * (fc - 1), 2 * (fc - 1) + 1):
                                    asm_sync(sy, j)
                        for j in (2 * (NFF - 1), 2 * (NFF - 1) + 1):
                            asm_sync(sy, j)

                    @b5.tensor
                    def _(t):
                        for fc in range(NFF):
                            t.wait_ge(s_xf, 16 * KC * (fc + 1))
                            for ns in range(NS):
                                n = NS * fc + ns + 1
                                t.wait_ge(s_wg, 32 * n)
                                if n > 1:
                                    t.wait_ge(s_gl, n - 1)
                                    t.wait_ge(s_iv, n - 1)
                                if ns == 0 and fc > 0:
                                    t.wait_ge(s_dsg, MCT * fc)
                                for kc in range(KC):
                                    for tq in range(4):
                                        inst = t.matmul(psf[0][:, tq * 512:(tq + 1) * 512],
                                                 wgb[n % 2][:, kc * 128:(kc + 1) * 128],
                                                 xnf[:, kc * FFT + tq * 512:
                                                     kc * FFT + (tq + 1) * 512],
                                                 start=(kc == 0), stop=(kc == KC - 1))
                                inst.then_inc(s_gm, 1)
                                for kc in range(KC):
                                    for tq in range(4):
                                        inst = t.matmul(psf[1][:, tq * 512:(tq + 1) * 512],
                                                 wub[n % 2][:, kc * 128:(kc + 1) * 128],
                                                 xnf[:, kc * FFT + tq * 512:
                                                     kc * FFT + (tq + 1) * 512],
                                                 start=(kc == 0), stop=(kc == KC - 1))
                                inst.then_inc(s_um, 1)
                            for mc in range(MCT):
                                n = MCT * fc + mc + 1
                                t.wait_ge(s_wd, 16 * n)
                                if mc == 0:
                                    t.wait_ge(s_iv, NS * (fc + 1))
                                    t.wait_ge(s_gl, NS * (fc + 1))
                                if n > 2:
                                    t.wait_ge(s_dsg, n - 2)
                                pd = psf[n % 2]
                                for ns in range(NS):
                                    for tq in range(4):
                                        inst = t.matmul(pd[:, tq * 512:(tq + 1) * 512],
                                                 wdb[n % 2][:, ns * 128:(ns + 1) * 128],
                                                 inter[:, ns * FFT + tq * 512:
                                                       ns * FFT + (tq + 1) * 512],
                                                 start=(ns == 0), stop=(ns == NS - 1))
                                inst.then_inc(s_dm, 1)

                    @b5.scalar
                    def _(sc):
                        for fc in range(NFF):
                            for ns in range(NS):
                                n = NS * fc + ns + 1
                                sc.wait_ge(s_gm, n)
                                if n > 1:
                                    sc.wait_ge(s_iv, n - 1)
                                sc.activation(gst[:], psf[0][:], AF.Gelu_apprx_tanh
                                              ).then_inc(s_gl, 1)
                            for mc in range(MCT):
                                n = MCT * fc + mc + 1
                                sc.wait_ge(s_dm, n)
                                if n > 2:
                                    sc.wait_ge(s_ca, 32 * (n - 2))
                                sc.copy(dstg[n % 2][:], psf[n % 2][:]).then_inc(s_dsg, 1)

                    @b5.vector
                    def _(v):
                        for fc in range(NFF):
                            for ns in range(NS):
                                n = NS * fc + ns + 1
                                v.wait_ge(s_um, n)
                                v.wait_ge(s_gl, n)
                                v.tensor_mul(inter[:, ns * FFT:(ns + 1) * FFT],
                                             gst[:], psf[1][:]).then_inc(s_iv, 1)
                            if fc > 0:
                                for j in (2 * (fc - 1), 2 * (fc - 1) + 1):
                                    asm_vec(v, j)
                        for j in (2 * (NFF - 1), 2 * (NFF - 1) + 1):
                            asm_vec(v, j)

                    @b5.gpsimd
                    def _(g):
                        for fc in range(NFF):
                            for mc in range(MCT):
                                n = MCT * fc + mc + 1
                                g.wait_ge(s_dsg, n)
                                for half in range(2):
                                    j = 2 * fc + half
                                    g.dma_start(out=Cb[j][mc * 128:(mc + 1) * 128, :],
                                                in_=dstg[n % 2][:, half * RST:
                                                                (half + 1) * RST],
                                                accum_op=ALU.add).then_inc(s_ca, 16)
                            g.wait_ge(s_ca, 32 * MCT * (fc + 1))
                            for half in range(2):
                                j = 2 * fc + half
                                g.collective_compute(
                                    "ReduceScatter", ALU.add,
                                    replica_groups=[list(range(NCORE))],
                                    ins=[Cb[j].opt()], outs=[Scb[j].opt()]
                                ).then_inc(s_cc)
                            if fc == NFF - 1:
                                g.memset(ccd_sb[:], 0.0)
                                g.dma_start(out=ccdum_in, in_=ccd_sb[:]
                                            ).then_inc(s_ccd, 16)
                                g.wait_ge(s_ccd, 16)
                                g.collective_compute(
                                    "ReduceScatter", ALU.add,
                                    replica_groups=[list(range(NCORE))],
                                    ins=[ccdum_in.opt()], outs=[ccdum_out.opt()]
                                ).then_inc(s_cc)
                            if fc > 0:
                                for j in (2 * (fc - 1), 2 * (fc - 1) + 1):
                                    asm_gps(g, j)
                        for j in (2 * (NFF - 1), 2 * (NFF - 1) + 1):
                            asm_gps(g, j)
                        g.wait_ge(s_ow, 16 * 2 * NRS)

        if dump_specs or want_kv:
            nc.all_engine_barrier()
            with nc.Block() as bd:
                @bd.gpsimd
                def _(g):
                    n = 0
                    for dst, src in dump_specs:
                        rows = src.shape[0]
                        step = 512
                        for r0 in range(0, rows, step):
                            r1 = min(rows, r0 + step)
                            n += 1
                            g.dma_start(out=dst[r0:r1, :],
                                        in_=src[r0:r1, :]).then_inc(s_dbg, 16)
                    g.wait_ge(s_dbg, 16 * n)
    return nc


def _rope_tables():
    inv = 1.0 / THETA ** (np.arange(0, HD, 2, dtype=np.float64) / HD)
    t = np.arange(S, dtype=np.float64)
    fr = np.concatenate([inv, inv])[:, None] * t[None, :]
    return np.cos(fr), np.sin(fr)


def _masks():
    r = np.arange(128)[:, None]
    c = np.arange(640)[None, :]
    mst = ((c >= r + 1) & (c <= r + 512)).astype(np.float32)
    med = np.zeros((4, 128, 512), np.float32)
    cc = np.arange(512)[None, :]
    for i in range(4):
        med[i] = ((cc <= 128 * i + r) & (cc < 128 * (i + 1))).astype(np.float32)
    return mst, med


def make_in_maps(L0_attn_x, L0_attn_Wq, L0_attn_Wk, L0_attn_Wv, L0_attn_Wo,
                 L0_attn_lam, L0_ffn_gate_W_packed, L0_ffn_gate_scales,
                 L0_ffn_up_W_packed, L0_ffn_up_scales, L0_ffn_down_W_packed,
                 L0_ffn_down_scales):
    xf = np.ascontiguousarray(np.asarray(L0_attn_x, dtype=np.float32).reshape(S, DM))
    cosd, sind = _rope_tables()
    qscale = 1.0 / np.sqrt(float(SUB))
    rcq = np.ascontiguousarray((cosd * qscale).astype(np.float32))
    rsq = np.ascontiguousarray((sind * qscale).astype(np.float32))
    rck = np.ascontiguousarray(cosd.astype(np.float32))
    rsk = np.ascontiguousarray(sind.astype(np.float32))
    mst, med = _masks()
    idb = np.eye(128, dtype=BF)
    Wq = np.asarray(L0_attn_Wq, np.float32)
    Wk = np.asarray(L0_attn_Wk, np.float32)
    Wv = np.asarray(L0_attn_Wv, np.float32)
    Wo = np.asarray(L0_attn_Wo, np.float32)
    lam = np.asarray(L0_attn_lam, np.float32)
    gp = np.asarray(L0_ffn_gate_W_packed, np.int32)
    gs = np.asarray(L0_ffn_gate_scales, np.float32)
    up = np.asarray(L0_ffn_up_W_packed, np.int32)
    us = np.asarray(L0_ffn_up_scales, np.float32)
    dp = np.asarray(L0_ffn_down_W_packed, np.int32)
    ds = np.asarray(L0_ffn_down_scales, np.float32)
    in_maps = []
    for i in range(NCORE):
        g = i // 2
        in_maps.append(dict(
            x=xf,
            xresT=np.ascontiguousarray(xf[:, 256 * i:256 * (i + 1)].T),
            wq=np.ascontiguousarray(Wq[:, 256 * i:256 * (i + 1)]),
            wk=np.ascontiguousarray(Wk[:, 128 * g:128 * (g + 1)]),
            wv=np.ascontiguousarray(Wv[:, 128 * g:128 * (g + 1)]),
            wo=np.ascontiguousarray(Wo[256 * i:256 * (i + 1), :]),
            lam2=np.ascontiguousarray(np.broadcast_to(
                lam[2 * i:2 * i + 2, None], (2, 128))),
            gpk=np.ascontiguousarray(gp[NLOC * i:NLOC * (i + 1), :]),
            gsc=np.ascontiguousarray(gs[NLOC * i:NLOC * (i + 1), :]),
            upk=np.ascontiguousarray(up[NLOC * i:NLOC * (i + 1), :]),
            usc=np.ascontiguousarray(us[NLOC * i:NLOC * (i + 1), :]),
            dpk=np.ascontiguousarray(dp[:, 1024 * i:1024 * (i + 1)]),
            dsc=np.ascontiguousarray(ds[:, 64 * i:64 * (i + 1)]),
            rcosq=rcq, rsinq=rsq, rcosk=rck, rsink=rsk,
            ident=idb, mask_st=mst, mask_ed=med,
        ))
    return in_maps


def kernel(**inputs):
    in_maps = make_in_maps(**inputs)
    nc = build_nc()
    res = run_bass_kernel_spmd(nc, in_maps, core_ids=list(range(NCORE)))
    rows = [np.asarray(res.results[i]["out"]) for i in range(NCORE)]
    outT = np.concatenate(rows, axis=0)
    return np.ascontiguousarray(outT.T).reshape(1, S, DM)


# version 09
# speedup vs baseline: 1.1164x; 1.1164x over previous
import numpy as np
import ml_dtypes
import concourse.bass as bass
import concourse.mybir as mybir
from concourse.bass_utils import run_bass_kernel_spmd
from contextlib import ExitStack

F32 = mybir.dt.float32
BF16 = mybir.dt.bfloat16
I32 = mybir.dt.int32
AF = mybir.ActivationFunctionType
ALU = mybir.AluOpType
AX = mybir.AxisListType

NCORE = 8
S = 8192
DM = 2048
HD = 128
SUB = 64
NQT = S // 128
TC5 = 512
NC5 = S // TC5
FFT = 2048
NFF = S // FFT
RST = 1024
NRS = S // RST
NLOC = 2048
KC = 16
NS = 16
MCT = 16
THETA = 10000.0
BF = ml_dtypes.bfloat16


def build_nc(dump=(), detect_races=True, max_phase=5, attn_mode='full',
             qt_list=None):
    dump = set(dump)
    nc = bass.Bass("TRN2", target_bir_lowering=False,
                   detect_race_conditions=detect_races)

    x = nc.dram_tensor("x", [S, DM], F32, kind="ExternalInput").ap()
    xresT = nc.dram_tensor("xresT", [256, S], F32, kind="ExternalInput").ap()
    wq = nc.dram_tensor("wq", [DM, 256], F32, kind="ExternalInput").ap()
    wk = nc.dram_tensor("wk", [DM, HD], F32, kind="ExternalInput").ap()
    wv = nc.dram_tensor("wv", [DM, HD], F32, kind="ExternalInput").ap()
    wo = nc.dram_tensor("wo", [256, DM], F32, kind="ExternalInput").ap()
    lam2 = nc.dram_tensor("lam2", [2, 128], F32, kind="ExternalInput").ap()
    gpk = nc.dram_tensor("gpk", [NLOC, 1024], I32, kind="ExternalInput").ap()
    gsc = nc.dram_tensor("gsc", [NLOC, 64], F32, kind="ExternalInput").ap()
    upk = nc.dram_tensor("upk", [NLOC, 1024], I32, kind="ExternalInput").ap()
    usc = nc.dram_tensor("usc", [NLOC, 64], F32, kind="ExternalInput").ap()
    dpk = nc.dram_tensor("dpk", [DM, 1024], I32, kind="ExternalInput").ap()
    dsc = nc.dram_tensor("dsc", [DM, 64], F32, kind="ExternalInput").ap()
    rcosq = nc.dram_tensor("rcosq", [128, S], F32, kind="ExternalInput").ap()
    rsinq = nc.dram_tensor("rsinq", [128, S], F32, kind="ExternalInput").ap()
    rcosk = nc.dram_tensor("rcosk", [128, S], F32, kind="ExternalInput").ap()
    rsink = nc.dram_tensor("rsink", [128, S], F32, kind="ExternalInput").ap()
    ident = nc.dram_tensor("ident", [128, 128], BF16, kind="ExternalInput").ap()
    mask_st = nc.dram_tensor("mask_st", [128, 640], F32, kind="ExternalInput").ap()
    mask_ed = nc.dram_tensor("mask_ed", [4, 128, 512], F32, kind="ExternalInput").ap()

    xnT = nc.dram_tensor("xnT", [DM, S], BF16).ap()
    qTd = nc.dram_tensor("qTd", [NQT, 2, 2, 64, 128], BF16).ap()
    wgTd = nc.dram_tensor("wgTd", [NS * KC, 128, 128], BF16).ap()
    wuTd = nc.dram_tensor("wuTd", [NS * KC, 128, 128], BF16).ap()
    wdTd = nc.dram_tensor("wdTd", [MCT * NS, 128, 128], BF16).ap()
    Cb = [nc.dram_tensor(f"Cbuf{j}", [DM, RST], F32).ap() for j in range(NRS)]
    ccdum_in = nc.dram_tensor("ccdum_in", [8, 128], F32).ap()
    ccdum_out = nc.dram_tensor("ccdum_out", [1, 128], F32).ap()
    Scb = [nc.dram_tensor(f"Scbuf{j}", [256, RST], F32).ap() for j in range(NRS)]

    out = nc.dram_tensor("out", [256, S], F32, kind="ExternalOutput").ap()

    dump_specs = []
    def add_dump(name, src, shape):
        d = nc.dram_tensor(name, shape, src.dtype if hasattr(src, "dtype") else BF16,
                           kind="ExternalOutput").ap()
        dump_specs.append((d, src))
        return d
    if "xnT" in dump:
        add_dump("d_xnT", xnT, [DM, S])
    if "qTd" in dump:
        add_dump("d_qTd", qTd.rearrange("a b s c d -> (a b s c) d"),
                 [NQT * 2 * 2 * 64, 128])
    if "wgTd" in dump:
        add_dump("d_wgTd", wgTd.rearrange("a b c -> (a b) c"), [NS * KC * 128, 128])
    if "wuTd" in dump:
        add_dump("d_wuTd", wuTd.rearrange("a b c -> (a b) c"), [NS * KC * 128, 128])
    if "wdTd" in dump:
        add_dump("d_wdTd", wdTd.rearrange("a b c -> (a b) c"), [MCT * NS * 128, 128])
    if "C" in dump:
        for j in range(NRS):
            add_dump(f"d_C{j}", Cb[j], [DM, RST])
    want_kv = "kv" in dump

    es = ExitStack()
    with es:
        T = lambda n, s, d: es.enter_context(nc.sbuf_tensor(n, s, d))
        SEM = lambda n: es.enter_context(nc.semaphore(n))

        cst = T("cst", [128, 8], F32)
        ident_sb = T("ident_sb", [128, 128], BF16)
        lam_sb = T("lam_sb", [128, 2], F32)

        for i, val in enumerate([7.5, -4.0, -6.0, -1.0, 1e-5, -0.5, 0.0, 1.0]):
            nc.const_aps.register(F32, val, cst[:, i:i + 1])

        ld0 = SEM("ld0"); cm0 = SEM("cm0")
        s_x = SEM("s_x"); s_st1 = SEM("s_st1"); s_sq = SEM("s_sq")
        s_rst = SEM("s_rst"); s_xn = SEM("s_xn"); s_tr = SEM("s_tr")
        s_ev = SEM("s_ev"); s_stg = SEM("s_stg"); s_w = SEM("s_w")
        s_xc = SEM("s_xc"); s_rl = SEM("s_rl"); s_mmq = SEM("s_mmq")
        s_mmk = SEM("s_mmk"); s_mmv = SEM("s_mmv"); s_rq = SEM("s_rq")
        s_rk = SEM("s_rk"); s_ve = SEM("s_ve"); s_qs = SEM("s_qs")
        s_pk = SEM("s_pk"); s_dv = SEM("s_dv"); s_ac = SEM("s_ac")
        s_tw = SEM("s_tw"); s_we = SEM("s_we"); s_ws = SEM("s_ws")
        s_qb = SEM("s_qb"); s_sc = SEM("s_sc"); s_sm = SEM("s_sm")
        s_ax = SEM("s_ax"); s_dt = SEM("s_dt"); s_de = SEM("s_de")
        s_av = SEM("s_av"); s_ae = SEM("s_ae"); s_wo = SEM("s_wo")
        s_ce = SEM("s_ce"); s_cw = SEM("s_cw")
        s_xf = SEM("s_xf"); s_wg = SEM("s_wg"); s_wd = SEM("s_wd")
        s_gm = SEM("s_gm"); s_um = SEM("s_um"); s_gl = SEM("s_gl")
        s_iv = SEM("s_iv"); s_dm = SEM("s_dm"); s_dsg = SEM("s_dsg")
        s_ca = SEM("s_ca"); s_cc = SEM("s_cc"); s_sl = SEM("s_sl")
        s_xr = SEM("s_xr"); s_oa = SEM("s_oa"); s_ow = SEM("s_ow")
        s_nm = SEM("s_nm"); s_ex = SEM("s_ex"); s_ch = SEM("s_ch")
        s_ccd = SEM("s_ccd")
        s_e2a = SEM("s_e2a"); s_dfn = SEM("s_dfn")
        s_dbg = SEM("s_dbg")

        attn_es = ExitStack()
        Ta = lambda n, s, d: attn_es.enter_context(nc.sbuf_tensor(n, s, d))
        mst_sb = Ta("mst_sb", [128, 640], F32)
        med_sb = Ta("med_sb", [128, 4 * 512], F32)
        kT1_sb = Ta("kT1_sb", [64, S], BF16)
        kT2_sb = Ta("kT2_sb", [64, S], BF16)
        v_sb = Ta("v_sb", [128, 64 * 128], BF16)
        wq_sb = Ta("wq_sb", [128, KC * 256], BF16)
        wk_sb = Ta("wk_sb", [128, KC * 128], BF16)
        wv_sb = Ta("wv_sb", [128, KC * 128], BF16)
        wo_sb = Ta("wo_sb", [128, 2 * DM], BF16)

        with nc.Block() as b0:
            @b0.vector
            def _(v):
                vals = [7.5, -4.0, -6.0, -1.0, 1e-5, -0.5, 0.0, 1.0]
                for i, val in enumerate(vals[:-1]):
                    v.memset(cst[:, i:i + 1], val)
                v.memset(cst[:, 7:8], vals[-1]).then_inc(cm0, 1)

            @b0.sync
            def _(sy):
                sy.dma_start(out=ident_sb[:], in_=ident).then_inc(ld0, 16)
                sy.dma_start(out=mst_sb[:], in_=mask_st).then_inc(ld0, 16)
                sy.dma_start(out=med_sb.ap().rearrange("p (e w) -> p e w", e=4),
                             in_=mask_ed.transpose([1, 0, 2])).then_inc(ld0, 16)
                sy.dma_start(out=lam_sb.ap().unsqueeze(2),
                             in_=lam2.transpose([1, 0]).unsqueeze(2),
                             allow_slow_non_contiguous=True).then_inc(ld0, 16)
        LD0 = 16 * 4
        nc.all_engine_barrier()

        if max_phase >= 1:
            with ExitStack() as ph:
                Tl = lambda n, s, d: ph.enter_context(nc.sbuf_tensor(n, s, d))
                Pl = lambda n, s, d: ph.enter_context(nc.psum_tensor(n, s, d))
                xb = [Tl(f"xb{i}", [128, DM], F32) for i in range(4)]
                sq_scr = Tl("sq_scr", [128, DM], F32)
                stat = Tl("stat", [128, 32], F32)
                xnb = [Tl(f"xnb{i}", [128, DM], BF16) for i in range(2)]
                xnstage = Tl("xnstage", [128, KC * TC5], BF16)
                ps_tr = [Pl(f"ps_tr{i}", [128, 128], BF16) for i in range(4)]

                with nc.Block() as b1:
                    @b1.gpsimd
                    def _(g):
                        for kc in range(KC):
                            g.dma_start(out=wq_sb[:, kc * 256:(kc + 1) * 256],
                                        in_=wq[kc * 128:(kc + 1) * 128, :]).then_inc(s_w, 16)
                            g.dma_start(out=wk_sb[:, kc * 128:(kc + 1) * 128],
                                        in_=wk[kc * 128:(kc + 1) * 128, :]).then_inc(s_w, 16)
                            g.dma_start(out=wv_sb[:, kc * 128:(kc + 1) * 128],
                                        in_=wv[kc * 128:(kc + 1) * 128, :]).then_inc(s_w, 16)
                        for c in range(2):
                            g.dma_start(out=wo_sb[:, c * DM:(c + 1) * DM],
                                        in_=wo[c * 128:(c + 1) * 128, :]).then_inc(s_w, 16)

                    @b1.sync
                    def _(sy):
                        for tc in range(NC5):
                            for s in range(4):
                                i = 4 * tc + s
                                if i >= 4:
                                    sy.wait_ge(s_xn, i - 3)
                                sy.dma_start(out=xb[s][:],
                                             in_=x[i * 128:(i + 1) * 128, :]
                                             ).then_inc(s_x, 16)
                            if tc > 0:
                                sy.wait_ge(s_ev, 64 * tc)
                                for kc in range(KC):
                                    sy.dma_start(
                                        out=xnT[kc * 128:(kc + 1) * 128,
                                                (tc - 1) * TC5:tc * TC5],
                                        in_=xnstage[:, kc * TC5:(kc + 1) * TC5]
                                        ).then_inc(s_stg, 16)
                        sy.wait_ge(s_ev, 64 * NC5)
                        for kc in range(KC):
                            sy.dma_start(out=xnT[kc * 128:(kc + 1) * 128, (NC5 - 1) * TC5:],
                                         in_=xnstage[:, kc * TC5:(kc + 1) * TC5]
                                         ).then_inc(s_stg, 16)
                        sy.wait_ge(s_stg, 16 * KC * NC5)

                    @b1.vector
                    def _(v):
                        v.wait_ge(cm0, 1)
                        for tc in range(NC5):
                            for s in range(4):
                                v.wait_ge(s_x, 16 * (4 * tc + s + 1))
                                v.reduce_sum(stat[:, 8 * s:8 * s + 1], xb[s][:], axis=AX.X)
                            for s in range(4):
                                v.tensor_scalar_mul(stat[:, 8 * s + 2:8 * s + 3],
                                                    stat[:, 8 * s:8 * s + 1], 1.0 / DM)
                            for s in range(4):
                                v.tensor_mul(stat[:, 8 * s + 3:8 * s + 4],
                                             stat[:, 8 * s + 2:8 * s + 3],
                                             stat[:, 8 * s + 2:8 * s + 3])
                            v.wait_ge(s_sq, 4 * (tc + 1))
                            for s in range(4):
                                v.scalar_tensor_tensor(
                                    out=stat[:, 8 * s + 4:8 * s + 5],
                                    in0=stat[:, 8 * s + 1:8 * s + 2], scalar=1.0 / DM,
                                    in1=stat[:, 8 * s + 3:8 * s + 4],
                                    op0=ALU.mult, op1=ALU.subtract).then_inc(s_st1, 1)
                            v.wait_ge(s_rst, 4 * (tc + 1))
                            for s in range(4):
                                v.scalar_tensor_tensor(
                                    out=stat[:, 8 * s + 3:8 * s + 4],
                                    in0=stat[:, 8 * s + 2:8 * s + 3], scalar=-1.0,
                                    in1=stat[:, 8 * s + 5:8 * s + 6],
                                    op0=ALU.mult, op1=ALU.mult)
                            for s in range(4):
                                i = 4 * tc + s
                                if i >= 2:
                                    v.wait_ge(s_tr, 16 * (i - 1))
                                v.tensor_scalar(
                                    out=xnb[s % 2][:], in0=xb[s][:],
                                    scalar1=stat[:, 8 * s + 5:8 * s + 6],
                                    scalar2=stat[:, 8 * s + 3:8 * s + 4],
                                    op0=ALU.mult, op1=ALU.add).then_inc(s_xn, 1)

                    @b1.scalar
                    def _(sc):
                        sc.wait_ge(cm0, 1)
                        for tc in range(NC5):
                            if tc > 0:
                                sc.wait_ge(s_st1, 4 * tc)
                            for s in range(4):
                                sc.wait_ge(s_x, 16 * (4 * tc + s + 1))
                                sc.activation(sq_scr[:], xb[s][:], AF.Square,
                                              accum_out=stat[:, 8 * s + 1:8 * s + 2]
                                              ).then_inc(s_sq, 1)
                            sc.wait_ge(s_st1, 4 * (tc + 1))
                            for s in range(4):
                                sc.activation(stat[:, 8 * s + 4:8 * s + 5],
                                              stat[:, 8 * s + 4:8 * s + 5],
                                              AF.Ln, bias=1e-5)
                            for s in range(4):
                                sc.activation(stat[:, 8 * s + 5:8 * s + 6],
                                              stat[:, 8 * s + 4:8 * s + 5],
                                              AF.Exp, scale=-0.5).then_inc(s_rst, 1)
                            if tc > 0:
                                sc.wait_ge(s_stg, 16 * KC * tc)
                            for s in range(4):
                                for kc in range(KC):
                                    n = 64 * tc + 16 * s + kc + 1
                                    sc.wait_ge(s_tr, n)
                                    sc.copy(xnstage[:, kc * TC5 + s * 128:
                                                    kc * TC5 + (s + 1) * 128],
                                            ps_tr[(n - 1) % 4][:]).then_inc(s_ev, 1)

                    @b1.tensor
                    def _(t):
                        t.wait_ge(ld0, LD0)
                        n = 0
                        for tc in range(NC5):
                            for s in range(4):
                                t.wait_ge(s_xn, 4 * tc + s + 1)
                                for kc in range(KC):
                                    n += 1
                                    if n > 4:
                                        t.wait_ge(s_ev, n - 4)
                                    t.transpose(ps_tr[(n - 1) % 4][:],
                                                xnb[s % 2][:, kc * 128:(kc + 1) * 128],
                                                ident_sb[:]).then_inc(s_tr, 1)
        nc.all_engine_barrier()

        if max_phase >= 2:
            with ExitStack() as ph:
                Tl = lambda n, s, d: ph.enter_context(nc.sbuf_tensor(n, s, d))
                Pl = lambda n, s, d: ph.enter_context(nc.psum_tensor(n, s, d))
                xns = [Tl(f"xns{i}", [128, KC * TC5], BF16) for i in range(2)]
                rcq = Tl("rcq", [128, TC5], F32)
                rsq = Tl("rsq", [128, TC5], F32)
                rck = Tl("rck", [128, TC5], F32)
                rsk = Tl("rsk", [128, TC5], F32)
                rtmp = [Tl(f"rtmp{i}", [64, TC5], F32) for i in range(4)]
                qkst = [Tl(f"qkst{i}", [128, TC5], BF16) for i in range(2)]
                ps_q = [Pl(f"ps_q{i}", [128, TC5], F32) for i in range(2)]
                ps_k = Pl("ps_k", [128, TC5], F32)
                ps_v = [Pl(f"ps_v{i}", [128, 128], F32) for i in range(2)]

                with nc.Block() as b2:
                    @b2.gpsimd
                    def _(g):
                        for tc in range(NC5):
                            for h in range(2):
                                g.wait_ge(s_rq, 2 * tc + h + 1)
                                for sb in range(2):
                                    g.dma_start(
                                        out=qTd[tc * 4:(tc + 1) * 4, h, sb]
                                        .transpose([1, 0, 2]),
                                        in_=qkst[h].ap()[sb * 64:(sb + 1) * 64, :]
                                        .rearrange("p (q f) -> p q f", q=4)
                                        ).then_inc(s_qs, 16)
                        g.wait_ge(s_qs, 32 * 2 * NC5)

                    @b2.sync
                    def _(sy):
                        for tc in range(NC5):
                            if tc >= 2:
                                sy.wait_ge(s_mmv, 4 * (tc - 1))
                            for kc in range(KC):
                                sy.dma_start(out=xns[tc % 2][:, kc * TC5:(kc + 1) * TC5],
                                             in_=xnT[kc * 128:(kc + 1) * 128,
                                                     tc * TC5:(tc + 1) * TC5]
                                             ).then_inc(s_xc, 16)
                            if tc >= 1:
                                sy.wait_ge(s_rq, 2 * tc)
                                sy.wait_ge(s_rk, tc)
                            sl = slice(tc * TC5, (tc + 1) * TC5)
                            sy.dma_start(out=rcq[:], in_=rcosq[:, sl]).then_inc(s_rl, 16)
                            sy.dma_start(out=rsq[:], in_=rsinq[:, sl]).then_inc(s_rl, 16)
                            sy.dma_start(out=rck[:], in_=rcosk[:, sl]).then_inc(s_rl, 16)
                            sy.dma_start(out=rsk[:], in_=rsink[:, sl]).then_inc(s_rl, 16)

                    @b2.tensor
                    def _(t):
                        t.wait_ge(s_w, 16 * (3 * KC + 2))
                        for tc in range(NC5):
                            t.wait_ge(s_xc, 16 * KC * (tc + 1))
                            for h in range(2):
                                if tc > 0:
                                    t.wait_ge(s_rq, 2 * (tc - 1) + h + 1)
                                for kc in range(KC):
                                    inst = t.matmul(ps_q[h][:],
                                             wq_sb[:, kc * 256 + h * 128:
                                                   kc * 256 + (h + 1) * 128],
                                             xns[tc % 2][:, kc * TC5:(kc + 1) * TC5],
                                             start=(kc == 0), stop=(kc == KC - 1))
                                inst.then_inc(s_mmq, 1)
                            if tc > 0:
                                t.wait_ge(s_rk, tc)
                            for kc in range(KC):
                                inst = t.matmul(ps_k[:],
                                         wk_sb[:, kc * 128:(kc + 1) * 128],
                                         xns[tc % 2][:, kc * TC5:(kc + 1) * TC5],
                                         start=(kc == 0), stop=(kc == KC - 1))
                            inst.then_inc(s_mmk, 1)
                            for s in range(4):
                                i = 4 * tc + s
                                if i >= 2:
                                    t.wait_ge(s_ve, i - 1)
                                for kc in range(KC):
                                    inst = t.matmul(ps_v[i % 2][:],
                                             xns[tc % 2][:, kc * TC5 + s * 128:
                                                          kc * TC5 + (s + 1) * 128],
                                             wv_sb[:, kc * 128:(kc + 1) * 128],
                                             start=(kc == 0), stop=(kc == KC - 1))
                                inst.then_inc(s_mmv, 1)

                    @b2.vector
                    def _(v):
                        def rope(dst_lo, dst_hi, ps, ct, st):
                            v.tensor_mul(rtmp[0][:], ps[0:64, :], ct[0:64, :])
                            v.tensor_mul(rtmp[1][:], ps[64:128, :], st[0:64, :])
                            v.tensor_mul(rtmp[2][:], ps[64:128, :], ct[64:128, :])
                            v.tensor_mul(rtmp[3][:], ps[0:64, :], st[64:128, :])
                            v.tensor_sub(dst_lo, rtmp[0][:], rtmp[1][:])
                            return v.tensor_add(dst_hi, rtmp[2][:], rtmp[3][:])

                        for tc in range(NC5):
                            v.wait_ge(s_rl, 16 * 4 * (tc + 1))
                            for h in range(2):
                                v.wait_ge(s_mmq, 2 * tc + h + 1)
                                if tc > 0:
                                    v.wait_ge(s_qs, 32 * (2 * (tc - 1) + h + 1))
                                rope(qkst[h][0:64, :], qkst[h][64:128, :],
                                     ps_q[h], rcq, rsq).then_inc(s_rq, 1)
                            v.wait_ge(s_mmk, tc + 1)
                            rope(kT1_sb[:, tc * TC5:(tc + 1) * TC5],
                                 kT2_sb[:, tc * TC5:(tc + 1) * TC5],
                                 ps_k, rck, rsk).then_inc(s_rk, 1)

                    @b2.scalar
                    def _(sc):
                        for tc in range(NC5):
                            for s in range(4):
                                i = 4 * tc + s
                                sc.wait_ge(s_mmv, i + 1)
                                sc.copy(v_sb[:, i * 128:(i + 1) * 128],
                                        ps_v[i % 2][:]).then_inc(s_ve, 1)
        nc.all_engine_barrier()

        if max_phase >= 3:
            with ExitStack() as ph:
                Tl = lambda n, s, d: ph.enter_context(nc.sbuf_tensor(n, s, d))
                Pl = lambda n, s, d: ph.enter_context(nc.psum_tensor(n, s, d))
                pkt = [Tl(f"pkt{i}", [128, 1024], I32) for i in range(2)]
                scs = [Tl(f"scs{i}", [128, 64], F32) for i in range(2)]
                lo_i = Tl("lo_i", [128, 1024], I32)
                hi_i = Tl("hi_i", [128, 1024], I32)
                cbf = Tl("cbf", [128, 2048], BF16)
                ble = Tl("ble", [128, 2048], BF16)
                sgn = Tl("sgn", [128, 2048], BF16)
                mtl = Tl("mtl", [128, 2048], BF16)
                rr1 = Tl("rr1", [128, 2048], BF16)
                rr2 = Tl("rr2", [128, 2048], BF16)
                fvl = Tl("fvl", [128, 2048], BF16)
                svl = Tl("svl", [128, 2048], BF16)
                wstg = [Tl(f"wstg{i}", [128, KC * 128], BF16) for i in range(2)]
                ps_w = [Pl(f"ps_w{i}", [128, 128], BF16) for i in range(4)]

                mats = [(gpk, gsc, wgTd), (upk, usc, wuTd), (dpk, dsc, wdTd)]
                NT = 16

                with nc.Block() as b3:
                    @b3.sync
                    def _(sy):
                        n = 0
                        for (pk, sc_, wd) in mats:
                            for tt in range(NT):
                                n += 1
                                if n > 2:
                                    sy.wait_ge(s_dv, 11 * (n - 2))
                                sy.dma_start(out=pkt[n % 2][:],
                                             in_=pk[tt * 128:(tt + 1) * 128, :]
                                             ).then_inc(s_pk, 16)
                                sy.dma_start(out=scs[n % 2][:],
                                             in_=sc_[tt * 128:(tt + 1) * 128, :]
                                             ).then_inc(s_pk, 16)
                                sy.wait_ge(s_we, 16 * n)
                                sy.dma_start(out=wd[tt * KC:(tt + 1) * KC]
                                             .transpose([1, 0, 2]),
                                             in_=wstg[n % 2].ap()
                                             .rearrange("p (b f) -> p b f", b=KC)
                                             ).then_inc(s_ws, 16)
                        sy.wait_ge(s_ws, 16 * 3 * NT)

                    @b3.vector
                    def _(v):
                        n = 0
                        for mi in range(3):
                            for tt in range(NT):
                                n += 1
                                b = 11 * (n - 1)
                                v.wait_ge(s_pk, 32 * n)
                                v.tensor_single_scalar(lo_i[:], pkt[n % 2][:], 15,
                                                       ALU.bitwise_and)
                                v.tensor_single_scalar(hi_i[:], pkt[n % 2][:], 4,
                                                       ALU.logical_shift_right
                                                       ).then_inc(s_dv, 1)
                                v.wait_ge(s_dv, b + 1)
                                v.tensor_copy(cbf[:, 0::2], lo_i[:])
                                v.tensor_copy(cbf[:, 1::2], hi_i[:]).then_inc(s_dv, 1)
                                v.wait_ge(s_dv, b + 2)
                                v.tensor_single_scalar(ble[:], cbf[:], 7.5,
                                                       ALU.is_le).then_inc(s_dv, 1)
                                v.wait_ge(s_dv, b + 3)
                                v.tensor_scalar(out=sgn[:], in0=ble[:],
                                                scalar1=2.0, scalar2=-1.0,
                                                op0=ALU.mult, op1=ALU.add
                                                ).then_inc(s_dv, 1)
                                v.wait_ge(s_dv, b + 4)
                                v.tensor_scalar(out=mtl[:], in0=sgn[:],
                                                scalar1=4.0, scalar2=-4.0,
                                                op0=ALU.mult, op1=ALU.add
                                                ).then_inc(s_dv, 1)
                                v.wait_ge(s_dv, b + 5)
                                v.tensor_add(mtl[:], cbf[:], mtl[:]).then_inc(s_dv, 1)
                                v.wait_ge(s_dv, b + 6)
                                v.tensor_scalar(out=rr1[:], in0=mtl[:],
                                                scalar1=-4.0, scalar2=0.0,
                                                op0=ALU.add, op1=ALU.max)
                                v.tensor_scalar(out=rr2[:], in0=mtl[:],
                                                scalar1=-6.0, scalar2=0.0,
                                                op0=ALU.add, op1=ALU.max
                                                ).then_inc(s_dv, 1)
                                v.wait_ge(s_dv, b + 7)
                                v.scalar_tensor_tensor(out=rr1[:], in0=rr1[:],
                                                       scalar=0.5, in1=rr2[:],
                                                       op0=ALU.mult, op1=ALU.add
                                                       ).then_inc(s_dv, 1)
                                v.wait_ge(s_dv, b + 8)
                                v.scalar_tensor_tensor(out=fvl[:], in0=mtl[:],
                                                       scalar=0.5, in1=rr1[:],
                                                       op0=ALU.mult, op1=ALU.add
                                                       ).then_inc(s_dv, 1)
                                v.wait_ge(s_dv, b + 9)
                                v.tensor_mul(fvl[:], fvl[:], sgn[:]).then_inc(s_dv, 1)
                                v.wait_ge(s_dv, b + 10)
                                if n > 1:
                                    v.wait_ge(s_tw, 16 * (n - 1))
                                v.tensor_tensor(
                                    out=svl.ap().rearrange("p (b k) -> p b k", b=64),
                                    in0=fvl.ap().rearrange("p (b k) -> p b k", b=64),
                                    in1=scs[n % 2].ap().unsqueeze(2)
                                        .broadcast_to([128, 64, 32]),
                                    op=ALU.mult).then_inc(s_dv, 1)

                    @b3.scalar
                    def _(sc):
                        n = 0
                        for mi in range(3):
                            for tt in range(NT):
                                n += 1
                                for kc in range(KC):
                                    k = KC * (n - 1) + kc + 1
                                    sc.wait_ge(s_tw, k)
                                    sc.copy(wstg[n % 2][:, kc * 128:(kc + 1) * 128],
                                            ps_w[(k - 1) % 4][:]).then_inc(s_we, 1)

                    @b3.tensor
                    def _(t):
                        n = 0
                        k = 0
                        for mi in range(3):
                            for tt in range(NT):
                                n += 1
                                t.wait_ge(s_dv, 11 * n)
                                if n > 2:
                                    t.wait_ge(s_ws, 16 * (n - 2))
                                for kc in range(KC):
                                    k += 1
                                    if k > 4:
                                        t.wait_ge(s_we, k - 4)
                                    t.transpose(ps_w[(k - 1) % 4][:],
                                                svl[:, kc * 128:(kc + 1) * 128],
                                                ident_sb[:]).then_inc(s_tw, 1)
        nc.all_engine_barrier()

        if max_phase >= 4:
            with ExitStack() as ph:
                Tl = lambda n, s, d: ph.enter_context(nc.sbuf_tensor(n, s, d))
                Pl = lambda n, s, d: ph.enter_context(nc.psum_tensor(n, s, d))
                qblk = [Tl(f"qblk{i}", [64, 512], BF16) for i in range(2)]
                e1 = [Tl(f"e1_{i}", [128, 640], F32) for i in range(2)]
                e2 = [Tl(f"e2_{i}", [128, 640], F32) for i in range(2)]
                e1m = [Tl(f"e1m_{i}", [128, 640], BF16) for i in range(2)]
                e2m = [Tl(f"e2m_{i}", [128, 640], BF16) for i in range(2)]
                e2a = [Tl(f"e2a_{i}", [128, 640], BF16) for i in range(2)]
                dra = [Tl(f"dra_{i}", [128, 640], BF16) for i in range(2)]
                dre = [Tl(f"dre_{i}", [128, 640], BF16) for i in range(2)]
                dfin = [Tl(f"dfin_{i}", [128, 640], BF16) for i in range(2)]
                dT = [Tl(f"dT_{i}", [128, 640], BF16) for i in range(2)]
                st2 = Tl("st2", [128, 16], F32)
                dscr = Tl("dscr", [128, 2], F32)
                avT = [Tl(f"avT{i}", [128, 2 * 512], BF16) for i in range(2)]
                cstg = [Tl(f"cstg{i}", [128, 512], F32) for i in range(2)]
                psm1 = [Pl(f"psm1_{i}", [128, 512], F32) for i in range(2)]
                psm2 = [Pl(f"psm2_{i}", [128, 512], F32) for i in range(2)]
                pstl = [Pl(f"pstl_{i}", [128, 512], F32) for i in range(2)]
                psad = [Pl(f"psad_{i}", [128, 512], F32) for i in range(2)]

                QTS = list(range(NQT)) if qt_list is None else list(qt_list)
                N = 2 * len(QTS)
                NG = len(QTS) // 4 if len(QTS) % 4 == 0 else 0

                def wparams(qt):
                    if qt < 4:
                        return (0, 128 * (qt + 1),
                                med_sb[:, qt * 512:qt * 512 + 128 * (qt + 1)])
                    return 128 * (qt - 4), 640, mst_sb[:, 0:640]

                def it(n):
                    qi = (n - 1) // 2
                    return qi, QTS[qi], (n - 1) % 2, (n - 1) % 2

                CH = 8

                with nc.Block() as b4:
                    @b4.gpsimd
                    def _(g):
                        for gp in range(16):
                            j, off = gp // 2, (gp % 2) * 512
                            for mc in range(MCT):
                                nn = MCT * gp + mc + 1
                                g.wait_ge(s_ce, nn)
                                g.dma_start(out=Cb[j][mc * 128:(mc + 1) * 128,
                                                      off:off + 512],
                                            in_=cstg[(nn - 1) % 2][:]).then_inc(s_cw, 16)
                        g.wait_ge(s_cw, 16 * MCT * 16)

                    @b4.sync
                    def _(sy):
                        for qi, qt in enumerate(QTS):
                            if qi >= 2:
                                sy.wait_ge(s_sc, 2 * (qi - 1))
                            sy.dma_start(out=qblk[qi % 2].ap()
                                         .rearrange("p (h s f) -> p h s f", h=2, s=2),
                                         in_=qTd[qt].transpose([2, 0, 1, 3])
                                         ).then_inc(s_qb, 16)

                    @b4.tensor
                    def _(t):
                        wo_groups = 0
                        for n in range(1, N + 3):
                            if n <= N:
                                qi, qt, h, st = it(n)
                                w0, W, _m = wparams(qt)
                                Wm = min(W, 512)
                                if h == 0:
                                    t.wait_ge(s_qb, 16 * (qi + 1))
                                if n > 2:
                                    t.wait_ge(s_ex, n - 2)
                                if st == 0 and wo_groups > 0:
                                    t.wait_ge(s_ce, MCT * wo_groups)
                                q = qblk[qi % 2]
                                for half, ps, kTh in ((0, psm1[st], kT1_sb),
                                                      (1, psm2[st], kT2_sb)):
                                    lh = q[:, (2 * h + half) * 128:
                                           (2 * h + half + 1) * 128]
                                    inst = t.matmul(ps[:, 0:Wm], lh,
                                                    kTh[:, w0:w0 + Wm],
                                                    start=True, stop=True)
                                    if W > 512:
                                        inst = t.matmul(
                                            pstl[st][:, half * 128:(half + 1) * 128],
                                            lh, kTh[:, w0 + 512:w0 + 640],
                                            start=True, stop=True)
                                inst.then_inc(s_sc, 1)
                            m = n - 1
                            if 1 <= m <= N:
                                qi, qt, h, st = it(m)
                                w0, W, _m2 = wparams(qt)
                                t.wait_ge(s_dfn, m)
                                if m > 2:
                                    t.wait_ge(s_de, m - 2)
                                    t.wait_ge(s_ae, m - 2)
                                for w in range(W // 128):
                                    inst = t.transpose(
                                        psad[st].ap().bitcast(BF16)
                                        [:, 256 + w * 128:256 + (w + 1) * 128],
                                        dfin[st][:, w * 128:(w + 1) * 128],
                                        ident_sb[:])
                                inst.then_inc(s_dt, 1)
                            m = n - 2
                            if 1 <= m <= N:
                                qi, qt, h, st = it(m)
                                w0, W, _m2 = wparams(qt)
                                t.wait_ge(s_de, m)
                                for w in range(W // 128):
                                    inst = t.matmul(
                                        psad[st][:, 0:128],
                                        v_sb[:, (w0 // 128 + w) * 128:
                                             (w0 // 128 + w + 1) * 128],
                                        dT[st][:, w * 128:(w + 1) * 128],
                                        start=(w == 0), stop=(w == W // 128 - 1))
                                inst.then_inc(s_av, 1)
                                if m % 8 == 0:
                                    gloc = m // 8 - 1
                                    t.wait_ge(s_ae, 8 * (gloc + 1))
                                    t.wait_ge(s_ex, min(N, 8 * gloc + 9))
                                    for mc in range(MCT):
                                        nwo = MCT * gloc + mc + 1
                                        if nwo > 2:
                                            t.wait_ge(s_ce, nwo - 2)
                                        pw = psm1[0] if nwo % 2 == 1 else psm2[0]
                                        for c in range(2):
                                            inst = t.matmul(
                                                pw[:],
                                                wo_sb[:, c * DM + mc * 128:
                                                      c * DM + (mc + 1) * 128],
                                                avT[gloc % 2][:, c * 512:(c + 1) * 512],
                                                start=(c == 0), stop=(c == 1))
                                        inst.then_inc(s_wo, 1)
                                    wo_groups += 1

                    @b4.vector
                    def _(v):
                        for n in range(1, N + 4):
                            m = n - 3
                            if 1 <= m <= N:
                                qi, qt, h, st = it(m)
                                gi = qi // 4
                                v.wait_ge(s_av, m)
                                if qi % 4 == 0 and h == 0 and gi >= 2:
                                    v.wait_ge(s_wo, MCT * (gi - 1))
                                v.tensor_copy(avT[gi % 2][:, h * 512 + (qi % 4) * 128:
                                                          h * 512 + (qi % 4 + 1) * 128],
                                              psad[st][:, 0:128]).then_inc(s_ae, 1)
                                if m % 8 == 0:
                                    gloc = m // 8 - 1
                                    for mc in range(MCT):
                                        nwo = MCT * gloc + mc + 1
                                        v.wait_ge(s_wo, nwo)
                                        if nwo > 2:
                                            v.wait_ge(s_cw, 16 * (nwo - 2))
                                        pw = psm1[0] if nwo % 2 == 1 else psm2[0]
                                        v.tensor_copy(cstg[(nwo - 1) % 2][:], pw[:]
                                                      ).then_inc(s_ce, 1)

                            if n <= N:
                                qi, qt, h, st = it(n)
                                w0, W, msk = wparams(qt)
                                c0 = 8 * (st)
                                v.wait_ge(s_sc, n)
                                if W > 512:
                                    v.reduce_max(st2[:, c0:c0 + 1], psm1[st][:],
                                                 axis=AX.X, negate=True)
                                    v.reduce_max(st2[:, c0 + 1:c0 + 2], psm2[st][:],
                                                 axis=AX.X, negate=True)
                                    v.reduce_max(st2[:, c0 + 2:c0 + 3],
                                                 pstl[st][:, 0:128], axis=AX.X,
                                                 negate=True)
                                    v.reduce_max(st2[:, c0 + 3:c0 + 4],
                                                 pstl[st][:, 128:256], axis=AX.X,
                                                 negate=True)
                                    v.tensor_tensor(st2[:, c0:c0 + 1],
                                                    st2[:, c0:c0 + 1],
                                                    st2[:, c0 + 2:c0 + 3], ALU.min
                                                    ).then_inc(s_nm, 1)
                                    v.tensor_tensor(st2[:, c0 + 1:c0 + 2],
                                                    st2[:, c0 + 1:c0 + 2],
                                                    st2[:, c0 + 3:c0 + 4], ALU.min
                                                    ).then_inc(s_nm, 1)
                                else:
                                    v.reduce_max(st2[:, c0:c0 + 1], psm1[st][:, 0:W],
                                                 axis=AX.X, negate=True
                                                 ).then_inc(s_nm, 1)
                                    v.reduce_max(st2[:, c0 + 1:c0 + 2],
                                                 psm2[st][:, 0:W],
                                                 axis=AX.X, negate=True
                                                 ).then_inc(s_nm, 1)
                            m = n - 1
                            if 1 <= m <= N:
                                qi, qt, h, st = it(m)
                                w0, W, msk = wparams(qt)
                                c0 = 8 * st
                                base = CH * (m - 1)
                                v.wait_ge(s_ex, m)
                                v.scalar_tensor_tensor(
                                    out=e1m[st][:, 0:W], in0=e1[st][:, 0:W],
                                    scalar=1.0, in1=msk, op0=ALU.mult, op1=ALU.mult,
                                    accum_out=st2[:, c0 + 4:c0 + 5])
                                v.scalar_tensor_tensor(
                                    out=e2m[st][:, 0:W], in0=e2[st][:, 0:W],
                                    scalar=1.0, in1=msk, op0=ALU.mult, op1=ALU.mult,
                                    accum_out=st2[:, c0 + 5:c0 + 6]).then_inc(s_ch, 1)
                                v.wait_ge(s_ch, base + 1)
                                v.reciprocal(st2[:, c0 + 6:c0 + 7],
                                             st2[:, c0 + 5:c0 + 6])
                                v.reciprocal(st2[:, c0 + 2:c0 + 3],
                                             st2[:, c0 + 4:c0 + 5]).then_inc(s_ch, 1)
                                v.wait_ge(s_ch, base + 2)
                                v.tensor_mul(st2[:, c0 + 6:c0 + 7],
                                             st2[:, c0 + 6:c0 + 7],
                                             lam_sb[:, h:h + 1]).then_inc(s_ch, 1)
                                v.wait_ge(s_ch, base + 3)
                                v.tensor_scalar_mul(e2a[st][:, 0:W], e2m[st][:, 0:W],
                                                    st2[:, c0 + 6:c0 + 7]
                                                    ).then_inc(s_ch, 1)
                                v.wait_ge(s_ch, base + 4)
                                v.scalar_tensor_tensor(
                                    out=dra[st][:, 0:W], in0=e1m[st][:, 0:W],
                                    scalar=st2[:, c0 + 2:c0 + 3], in1=e2a[st][:, 0:W],
                                    op0=ALU.mult, op1=ALU.subtract).then_inc(s_ch, 1)
                                v.wait_ge(s_ch, base + 5)
                                v.tensor_scalar(out=dre[st][:, 0:W],
                                                in0=dra[st][:, 0:W],
                                                scalar1=0.0, scalar2=None,
                                                op0=ALU.max, op1=ALU.add,
                                                accum_out=st2[:, c0 + 7:c0 + 8]
                                                ).then_inc(s_ch, 1)
                                v.wait_ge(s_ch, base + 6)
                                v.tensor_scalar(out=dscr[:, 1:2],
                                                in0=st2[:, c0 + 7:c0 + 8],
                                                scalar1=1e-6, scalar2=None,
                                                op0=ALU.add).then_inc(s_ch, 1)
                                v.wait_ge(s_ch, base + 7)
                                v.reciprocal(st2[:, c0 + 7:c0 + 8],
                                             dscr[:, 1:2]).then_inc(s_ch, 1)
                                v.wait_ge(s_ch, base + 8)
                                if m > 2:
                                    v.wait_ge(s_dt, m - 2)
                                v.tensor_scalar_mul(dfin[st][:, 0:W], dre[st][:, 0:W],
                                                    st2[:, c0 + 7:c0 + 8]
                                                    ).then_inc(s_dfn, 1)
                            m = n - 2
                            if 1 <= m <= N:
                                qi, qt, h, st = it(m)
                                w0, W, _m2 = wparams(qt)
                                v.wait_ge(s_dt, m)
                                if m > 2:
                                    v.wait_ge(s_av, m - 2)
                                v.tensor_copy(dT[st][:, 0:W],
                                              psad[st].ap().bitcast(BF16)
                                              [:, 256:256 + W]).then_inc(s_de, 1)
                    @b4.scalar
                    def _(sc):
                        for n in range(1, N + 2):
                            if n <= N:
                                qi, qt, h, st = it(n)
                                w0, W, msk = wparams(qt)
                                c0 = 8 * st
                                Wm = min(W, 512)
                                sc.wait_ge(s_nm, 2 * n)
                                if n > 2:
                                    sc.wait_ge(s_ch, CH * (n - 3) + 1)
                                sc.activation(e1[st][:, 0:Wm], psm1[st][:, 0:Wm],
                                              AF.Exp, bias=st2[:, c0:c0 + 1])
                                inst = sc.activation(e2[st][:, 0:Wm],
                                                     psm2[st][:, 0:Wm],
                                                     AF.Exp, bias=st2[:, c0 + 1:c0 + 2])
                                if W > 512:
                                    sc.activation(e1[st][:, 512:640],
                                                  pstl[st][:, 0:128],
                                                  AF.Exp, bias=st2[:, c0:c0 + 1])
                                    inst = sc.activation(e2[st][:, 512:640],
                                                         pstl[st][:, 128:256],
                                                         AF.Exp,
                                                         bias=st2[:, c0 + 1:c0 + 2])
                                inst.then_inc(s_ex, 1)

        nc.all_engine_barrier()
        attn_es.close()

        if max_phase >= 5:
            with ExitStack() as ph:
                Tl = lambda n, s, d: ph.enter_context(nc.sbuf_tensor(n, s, d))
                Pl = lambda n, s, d: ph.enter_context(nc.psum_tensor(n, s, d))
                xnf = Tl("xnf", [128, KC * FFT], BF16)
                inter = Tl("inter", [128, NS * FFT], BF16)
                gst = Tl("gst", [128, FFT], BF16)
                wgb = [Tl(f"wgb{i}", [128, KC * 128], BF16) for i in range(2)]
                wub = [Tl(f"wub{i}", [128, KC * 128], BF16) for i in range(2)]
                wdb = [Tl(f"wdb{i}", [128, NS * 128], BF16) for i in range(2)]
                dstg = [Tl(f"dstg{i}", [128, FFT], F32) for i in range(2)]
                ccd_sb = Tl("ccd_sb", [8, 128], F32)
                ssb = [Tl(f"ssb{i}", [128, RST], F32) for i in range(2)]
                xq = [Tl(f"xq{i}", [128, RST], F32) for i in range(2)]
                osb = [Tl(f"osb{i}", [128, RST], F32) for i in range(2)]
                psf = [Pl(f"psf{i}", [128, FFT], F32) for i in range(2)]

                def asm_sync(sy, j):
                    sy.wait_ge(s_cc, j + 2)
                    for mt in range(2):
                        n = 2 * j + mt + 1
                        if n > 2:
                            sy.wait_ge(s_oa, n - 2)
                        sy.dma_start(out=ssb[n % 2][:],
                                     in_=Scb[j][mt * 128:(mt + 1) * 128, :]
                                     ).then_inc(s_sl, 16)
                        sy.dma_start(out=xq[n % 2][:],
                                     in_=xresT[mt * 128:(mt + 1) * 128,
                                               j * RST:(j + 1) * RST]
                                     ).then_inc(s_xr, 16)

                def asm_vec(v, j):
                    for mt in range(2):
                        n = 2 * j + mt + 1
                        v.wait_ge(s_sl, 16 * n)
                        v.wait_ge(s_xr, 16 * n)
                        if n > 2:
                            v.wait_ge(s_ow, 16 * (n - 2))
                        v.tensor_add(osb[n % 2][:], ssb[n % 2][:], xq[n % 2][:]
                                     ).then_inc(s_oa, 1)

                def asm_gps(g, j):
                    for mt in range(2):
                        n = 2 * j + mt + 1
                        g.wait_ge(s_oa, n)
                        g.dma_start(out=out[mt * 128:(mt + 1) * 128,
                                            j * RST:(j + 1) * RST],
                                    in_=osb[n % 2][:]).then_inc(s_ow, 16)

                with nc.Block() as b5:
                    @b5.sync
                    def _(sy):
                        for fc in range(NFF):
                            if fc > 0:
                                sy.wait_ge(s_um, NS * fc)
                            for kc in range(KC):
                                sy.dma_start(out=xnf[:, kc * FFT:(kc + 1) * FFT],
                                             in_=xnT[kc * 128:(kc + 1) * 128,
                                                     fc * FFT:(fc + 1) * FFT]
                                             ).then_inc(s_xf, 16)
                            for ns in range(NS):
                                n = NS * fc + ns + 1
                                if n > 2:
                                    sy.wait_ge(s_um, n - 2)
                                sy.dma_start(out=wgb[n % 2].ap()
                                             .rearrange("p (b f) -> p b f", b=KC),
                                             in_=wgTd[ns * KC:(ns + 1) * KC]
                                             .transpose([1, 0, 2])
                                             ).then_inc(s_wg, 16)
                                sy.dma_start(out=wub[n % 2].ap()
                                             .rearrange("p (b f) -> p b f", b=KC),
                                             in_=wuTd[ns * KC:(ns + 1) * KC]
                                             .transpose([1, 0, 2])
                                             ).then_inc(s_wg, 16)
                            for mc in range(MCT):
                                n = MCT * fc + mc + 1
                                if n > 2:
                                    sy.wait_ge(s_dm, n - 2)
                                sy.dma_start(out=wdb[n % 2].ap()
                                             .rearrange("p (b f) -> p b f", b=NS),
                                             in_=wdTd[mc * NS:(mc + 1) * NS]
                                             .transpose([1, 0, 2])
                                             ).then_inc(s_wd, 16)
                            if fc > 0:
                                for j in (2 * (fc - 1), 2 * (fc - 1) + 1):
                                    asm_sync(sy, j)
                        for j in (2 * (NFF - 1), 2 * (NFF - 1) + 1):
                            asm_sync(sy, j)

                    @b5.tensor
                    def _(t):
                        for fc in range(NFF):
                            t.wait_ge(s_xf, 16 * KC * (fc + 1))
                            for ns in range(NS):
                                n = NS * fc + ns + 1
                                t.wait_ge(s_wg, 32 * n)
                                if n > 1:
                                    t.wait_ge(s_gl, n - 1)
                                    t.wait_ge(s_iv, n - 1)
                                if ns == 0 and fc > 0:
                                    t.wait_ge(s_dsg, MCT * fc)
                                for kc in range(KC):
                                    for tq in range(4):
                                        inst = t.matmul(psf[0][:, tq * 512:(tq + 1) * 512],
                                                 wgb[n % 2][:, kc * 128:(kc + 1) * 128],
                                                 xnf[:, kc * FFT + tq * 512:
                                                     kc * FFT + (tq + 1) * 512],
                                                 start=(kc == 0), stop=(kc == KC - 1))
                                inst.then_inc(s_gm, 1)
                                for kc in range(KC):
                                    for tq in range(4):
                                        inst = t.matmul(psf[1][:, tq * 512:(tq + 1) * 512],
                                                 wub[n % 2][:, kc * 128:(kc + 1) * 128],
                                                 xnf[:, kc * FFT + tq * 512:
                                                     kc * FFT + (tq + 1) * 512],
                                                 start=(kc == 0), stop=(kc == KC - 1))
                                inst.then_inc(s_um, 1)
                            for mc in range(MCT):
                                n = MCT * fc + mc + 1
                                t.wait_ge(s_wd, 16 * n)
                                if mc == 0:
                                    t.wait_ge(s_iv, NS * (fc + 1))
                                    t.wait_ge(s_gl, NS * (fc + 1))
                                if n > 2:
                                    t.wait_ge(s_dsg, n - 2)
                                pd = psf[n % 2]
                                for ns in range(NS):
                                    for tq in range(4):
                                        inst = t.matmul(pd[:, tq * 512:(tq + 1) * 512],
                                                 wdb[n % 2][:, ns * 128:(ns + 1) * 128],
                                                 inter[:, ns * FFT + tq * 512:
                                                       ns * FFT + (tq + 1) * 512],
                                                 start=(ns == 0), stop=(ns == NS - 1))
                                inst.then_inc(s_dm, 1)

                    @b5.scalar
                    def _(sc):
                        for fc in range(NFF):
                            for ns in range(NS):
                                n = NS * fc + ns + 1
                                sc.wait_ge(s_gm, n)
                                if n > 1:
                                    sc.wait_ge(s_iv, n - 1)
                                sc.activation(gst[:], psf[0][:], AF.Gelu_apprx_tanh
                                              ).then_inc(s_gl, 1)
                            for mc in range(MCT):
                                n = MCT * fc + mc + 1
                                sc.wait_ge(s_dm, n)
                                if n > 2:
                                    sc.wait_ge(s_ca, 32 * (n - 2))
                                sc.copy(dstg[n % 2][:], psf[n % 2][:]).then_inc(s_dsg, 1)

                    @b5.vector
                    def _(v):
                        for fc in range(NFF):
                            for ns in range(NS):
                                n = NS * fc + ns + 1
                                v.wait_ge(s_um, n)
                                v.wait_ge(s_gl, n)
                                v.tensor_mul(inter[:, ns * FFT:(ns + 1) * FFT],
                                             gst[:], psf[1][:]).then_inc(s_iv, 1)
                            if fc > 0:
                                for j in (2 * (fc - 1), 2 * (fc - 1) + 1):
                                    asm_vec(v, j)
                        for j in (2 * (NFF - 1), 2 * (NFF - 1) + 1):
                            asm_vec(v, j)

                    @b5.gpsimd
                    def _(g):
                        for fc in range(NFF):
                            for mc in range(MCT):
                                n = MCT * fc + mc + 1
                                g.wait_ge(s_dsg, n)
                                for half in range(2):
                                    j = 2 * fc + half
                                    g.dma_start(out=Cb[j][mc * 128:(mc + 1) * 128, :],
                                                in_=dstg[n % 2][:, half * RST:
                                                                (half + 1) * RST],
                                                accum_op=ALU.add).then_inc(s_ca, 16)
                            g.wait_ge(s_ca, 32 * MCT * (fc + 1))
                            for half in range(2):
                                j = 2 * fc + half
                                g.collective_compute(
                                    "ReduceScatter", ALU.add,
                                    replica_groups=[list(range(NCORE))],
                                    ins=[Cb[j].opt()], outs=[Scb[j].opt()]
                                ).then_inc(s_cc)
                            if fc == NFF - 1:
                                g.memset(ccd_sb[:], 0.0)
                                g.dma_start(out=ccdum_in, in_=ccd_sb[:]
                                            ).then_inc(s_ccd, 16)
                                g.wait_ge(s_ccd, 16)
                                g.collective_compute(
                                    "ReduceScatter", ALU.add,
                                    replica_groups=[list(range(NCORE))],
                                    ins=[ccdum_in.opt()], outs=[ccdum_out.opt()]
                                ).then_inc(s_cc)
                            if fc > 0:
                                for j in (2 * (fc - 1), 2 * (fc - 1) + 1):
                                    asm_gps(g, j)
                        for j in (2 * (NFF - 1), 2 * (NFF - 1) + 1):
                            asm_gps(g, j)
                        g.wait_ge(s_ow, 16 * 2 * NRS)

        if dump_specs or want_kv:
            nc.all_engine_barrier()
            with nc.Block() as bd:
                @bd.gpsimd
                def _(g):
                    n = 0
                    for dst, src in dump_specs:
                        rows = src.shape[0]
                        step = 512
                        for r0 in range(0, rows, step):
                            r1 = min(rows, r0 + step)
                            n += 1
                            g.dma_start(out=dst[r0:r1, :],
                                        in_=src[r0:r1, :]).then_inc(s_dbg, 16)
                    g.wait_ge(s_dbg, 16 * n)
    return nc


def _rope_tables():
    inv = 1.0 / THETA ** (np.arange(0, HD, 2, dtype=np.float64) / HD)
    t = np.arange(S, dtype=np.float64)
    fr = np.concatenate([inv, inv])[:, None] * t[None, :]
    return np.cos(fr), np.sin(fr)


def _masks():
    r = np.arange(128)[:, None]
    c = np.arange(640)[None, :]
    mst = ((c >= r + 1) & (c <= r + 512)).astype(np.float32)
    med = np.zeros((4, 128, 512), np.float32)
    cc = np.arange(512)[None, :]
    for i in range(4):
        med[i] = ((cc <= 128 * i + r) & (cc < 128 * (i + 1))).astype(np.float32)
    return mst, med


def make_in_maps(L0_attn_x, L0_attn_Wq, L0_attn_Wk, L0_attn_Wv, L0_attn_Wo,
                 L0_attn_lam, L0_ffn_gate_W_packed, L0_ffn_gate_scales,
                 L0_ffn_up_W_packed, L0_ffn_up_scales, L0_ffn_down_W_packed,
                 L0_ffn_down_scales):
    xf = np.ascontiguousarray(np.asarray(L0_attn_x, dtype=np.float32).reshape(S, DM))
    cosd, sind = _rope_tables()
    qscale = 1.0 / np.sqrt(float(SUB))
    rcq = np.ascontiguousarray((cosd * qscale).astype(np.float32))
    rsq = np.ascontiguousarray((sind * qscale).astype(np.float32))
    rck = np.ascontiguousarray(cosd.astype(np.float32))
    rsk = np.ascontiguousarray(sind.astype(np.float32))
    mst, med = _masks()
    idb = np.eye(128, dtype=BF)
    Wq = np.asarray(L0_attn_Wq, np.float32)
    Wk = np.asarray(L0_attn_Wk, np.float32)
    Wv = np.asarray(L0_attn_Wv, np.float32)
    Wo = np.asarray(L0_attn_Wo, np.float32)
    lam = np.asarray(L0_attn_lam, np.float32)
    gp = np.asarray(L0_ffn_gate_W_packed, np.int32)
    gs = np.asarray(L0_ffn_gate_scales, np.float32)
    up = np.asarray(L0_ffn_up_W_packed, np.int32)
    us = np.asarray(L0_ffn_up_scales, np.float32)
    dp = np.asarray(L0_ffn_down_W_packed, np.int32)
    ds = np.asarray(L0_ffn_down_scales, np.float32)
    in_maps = []
    for i in range(NCORE):
        g = i // 2
        in_maps.append(dict(
            x=xf,
            xresT=np.ascontiguousarray(xf[:, 256 * i:256 * (i + 1)].T),
            wq=np.ascontiguousarray(Wq[:, 256 * i:256 * (i + 1)]),
            wk=np.ascontiguousarray(Wk[:, 128 * g:128 * (g + 1)]),
            wv=np.ascontiguousarray(Wv[:, 128 * g:128 * (g + 1)]),
            wo=np.ascontiguousarray(Wo[256 * i:256 * (i + 1), :]),
            lam2=np.ascontiguousarray(np.broadcast_to(
                lam[2 * i:2 * i + 2, None], (2, 128))),
            gpk=np.ascontiguousarray(gp[NLOC * i:NLOC * (i + 1), :]),
            gsc=np.ascontiguousarray(gs[NLOC * i:NLOC * (i + 1), :]),
            upk=np.ascontiguousarray(up[NLOC * i:NLOC * (i + 1), :]),
            usc=np.ascontiguousarray(us[NLOC * i:NLOC * (i + 1), :]),
            dpk=np.ascontiguousarray(dp[:, 1024 * i:1024 * (i + 1)]),
            dsc=np.ascontiguousarray(ds[:, 64 * i:64 * (i + 1)]),
            rcosq=rcq, rsinq=rsq, rcosk=rck, rsink=rsk,
            ident=idb, mask_st=mst, mask_ed=med,
        ))
    return in_maps


def kernel(**inputs):
    in_maps = make_in_maps(**inputs)
    nc = build_nc()
    res = run_bass_kernel_spmd(nc, in_maps, core_ids=list(range(NCORE)))
    rows = [np.asarray(res.results[i]["out"]) for i in range(NCORE)]
    outT = np.concatenate(rows, axis=0)
    return np.ascontiguousarray(outT.T).reshape(1, S, DM)
